# version 05
# baseline (speedup 1.0000x reference)
import contextlib
import numpy as np
import concourse.bass as bass
import concourse.mybir as mybir
from concourse.bass_utils import run_bass_kernel_spmd

F32 = mybir.dt.float32
BF16 = mybir.dt.bfloat16
AF = mybir.ActivationFunctionType
ALU = mybir.AluOpType
AXX = mybir.AxisListType.X

P = 128
T = 1024
NT = 8
D = 2048
KT = 16
II = 1408
IT = 11
NE = 9
DIN = 96
DST = 16
DTR = 6
EPS = 1e-5
NCORES = 8

ENGS = ("tensor", "vector", "scalar", "sync", "gpsimd")


class Sem:
    def __init__(self, es, nc, name):
        self.h = es.enter_context(nc.semaphore(name))
        self.n = 0

    def bump(self, k=1):
        self.n += k
        return self.n


class Plan:
    def __init__(self):
        self.q = {k: [] for k in ENGS}

    def wait(self, eng, sem, val):
        if val <= 0:
            return
        self.q[eng].append(lambda e, s=sem.h, v=val: e.wait_ge(s, v))

    def op(self, eng, build, inc=None, inc2=None):
        if inc is None:
            self.q[eng].append(build)
            return None
        sem, k = inc
        v = sem.bump(k)
        self.q[eng].append(lambda e, b=build, s=sem.h, kk=k: b(e).then_inc(s, kk))
        if inc2 is not None:
            sem2, k2 = inc2
            sem2.bump(k2)
            self.q[eng].append(
                lambda e, s2=sem2.h, kk2=k2: e.nop().then_inc(s2, kk2))
        return v

    def raw(self, eng, fn):
        self.q[eng].append(fn)


def build_program(do_mamba=True, do_ffn2=True, n_experts=NE):
    nc = bass.Bass("TRN2", target_bir_lowering=False)

    def din(name, shape):
        return nc.dram_tensor(name, list(shape), F32, kind="ExternalInput").ap()

    dx = din("x", [T, D])
    dwg = din("wgT", [NE, D, II])
    dwu = din("wuT", [NE, D, II])
    dwd = din("wdT", [NE, II, D])
    dgate = din("gateT", [D, 8])
    dinp = din("inpT", [D, DIN])
    dxpB = din("xpB", [DIN, DST])
    dxpC = din("xpC", [DIN, DST])
    dxpd = din("xpd", [DIN, DTR])
    ddtT = din("dtT", [DTR, DIN])
    ddtb = din("dtb", [DIN, 1])
    dconvw = din("convw", [DIN, 3])
    dconvb = din("convb", [DIN, 1])
    dAlog = din("Alog", [DIN, DST])
    dDp = din("Dp", [DIN, 1])
    doutT = din("outT", [DIN, D])
    dlnw = din("lnw_bc", [P, D])
    dlnb = din("lnb_bc", [P, D])
    dident = din("ident", [P, P])
    dselc = din("selc", [DIN, 3])
    dselL = din("selL", [DIN, 4])
    dselR = din("selR", [DIN, 4])
    dy = nc.dram_tensor("y", [T, D], F32, kind="ExternalOutput").ap()
    dag1i = nc.dram_tensor("ag1i", [DIN, 2], F32, kind="Internal").ap()
    dag1o = nc.dram_tensor("ag1o", [4 * DIN, 2], F32, kind="Internal").ap()
    dag2i = nc.dram_tensor("ag2i", [DIN, 2 * DST], F32, kind="Internal").ap()
    dag2o = nc.dram_tensor("ag2o", [4 * DIN, 2 * DST], F32, kind="Internal").ap()
    dbpcp = nc.dram_tensor("bpcp", [2 * DST, T], F32, kind="Internal").ap()
    groups = [[0, 1, 2, 3], [4, 5, 6, 7]]

    pl = Plan()
    es = contextlib.ExitStack()
    with es:
        sb = lambda name, shape, dt: es.enter_context(
            nc.sbuf_tensor(name, list(shape), dt))
        xr = sb("xr", [P, NT * D], F32)
        lnT = sb("lnT", [P, KT * T], BF16)
        wg2 = sb("wg2", [P, 3 * KT * P], BF16)
        wu2 = sb("wu2", [P, 3 * KT * P], BF16)
        ident_s = sb("ident_s", [P, P], BF16)
        gate_s = sb("gate_s", [P, KT * 8], BF16)
        sg2 = sb("sg2", [P, 2 * 512], F32)
        lg_s = sb("lg_s", [P, NT * 8], F32)
        m1_s = sb("m1_s", [P, NT], F32)
        m2_s = sb("m2_s", [P, NT], F32)
        e2_s = sb("e2_s", [P, NT], F32)
        w1_s = sb("w1_s", [P, NT], F32)
        w2_s = sb("w2_s", [P, NT], F32)
        eq1_s = sb("eq1_s", [P, NT * 8], F32)
        msk_s = sb("msk_s", [P, NT * 8], F32)
        comb = sb("comb", [P, NT * 8], F32)
        fsum = sb("fsum", [P, NT], F32)
        fsq = sb("fsq", [P, NT], F32)
        fmu = sb("fmu", [P, NT], F32)
        fvar = sb("fvar", [P, NT], F32)
        ftmp = sb("ftmp", [P, NT], F32)
        frstd = sb("frstd", [P, NT], F32)
        fnmu = sb("fnmu", [P, NT], F32)
        mmu = sb("mmu", [P, NT], F32)
        mvar = sb("mvar", [P, NT], F32)
        mrstd = sb("mrstd", [P, NT], F32)
        mtmp = sb("mtmp", [P, NT], F32)
        mseff = sb("mseff", [P, NT], F32)
        mnb = sb("mnb", [P, NT], F32)
        mnmr = sb("mnmr", [P, NT], F32)
        xpB_s = sb("xpB_s", [DIN, DST], F32)
        xpC_s = sb("xpC_s", [DIN, DST], F32)
        xpd_s = sb("xpd_s", [DIN, DTR], F32)
        dtT_s = sb("dtT_s", [DTR, DIN], F32)
        dtb_s = sb("dtb_s", [DIN, 1], F32)
        convw_s = sb("convw_s", [DIN, 3], F32)
        convb_s = sb("convb_s", [DIN, 1], F32)
        Alog_s = sb("Alog_s", [DIN, DST], F32)
        negA = sb("negA", [DIN, DST], F32)
        Dp_s = sb("Dp_s", [DIN, 1], F32)
        selc_s = sb("selc_s", [DIN, 3], F32)
        selL_s = sb("selL_s", [DIN, 4], F32)
        selR_s = sb("selR_s", [DIN, 4], F32)
        inp_s = sb("inp_s", [P, KT * DIN], BF16)
        outT_s = sb("outT_s", [DIN, D], BF16)
        F_s = sb("F_s", [DIN, DST], F32)
        Atot_s = sb("Atot_s", [DIN, DST], F32)
        carry = sb("carry", [DIN, DST], F32)
        cs1 = sb("cs1", [DIN, DST], F32)
        cs2 = sb("cs2", [DIN, DST], F32)
        cs3 = sb("cs3", [DIN, DST], F32)
        ctmp = sb("ctmp", [DIN, DST], F32)
        slots2 = sb("slots2", [DIN, 4 * 2 * DST], F32)
        slots1 = sb("slots1", [DIN, 4 * 2], F32)
        Stot_s = sb("Stot_s", [DIN, 1], F32)

        pA = es.enter_context(nc.psum_tensor("pA", [P, 2048], F32))
        pB = es.enter_context(nc.psum_tensor("pB", [P, 1024], F32))
        pC = es.enter_context(nc.psum_tensor("pC", [P, 512], F32))
        pD = es.enter_context(nc.psum_tensor("pD", [P, 512], F32))
        pB16 = pB[:].bitcast(BF16)
        pC16 = pC[:].bitcast(BF16)
        pD16 = pD[:].bitcast(BF16)
        tslots = [pB16[:, 0:P], pB16[:, 1024:1024 + P],
                  pC16[:, 0:P], pD16[:, 2 * P:3 * P]]

        S = {}
        for name in ("ld", "gld", "wg", "wgf", "wu", "wuf", "wd", "wdf",
                     "pg", "pu", "sg", "h", "pd", "xw", "trp", "tre",
                     "lntok", "rt", "rcp", "stat",
                     "mA", "mB", "mC", "mD", "mE", "mF", "mG", "mH",
                     "bcl", "bcf", "ag", "st", "mY"):
            S[name] = Sem(es, nc, "s_" + name)

        sync_loads = [
            (xr[:].rearrange("p (a f) -> p a f", a=NT),
             dx.rearrange("(a p) f -> p a f", p=P)),
            (xpB_s[:], dxpB), (xpC_s[:], dxpC), (xpd_s[:], dxpd),
            (dtT_s[:], ddtT), (dtb_s[:], ddtb), (convw_s[:], dconvw),
            (convb_s[:], dconvb), (Alog_s[:], dAlog), (Dp_s[:], dDp),
            (selc_s[:], dselc), (selL_s[:], dselL), (selR_s[:], dselR),
        ]
        for dst, src in sync_loads:
            pl.op("sync", lambda e, d_=dst, s_=src: e.dma_start(out=d_, in_=s_),
                  inc=(S["ld"], 16))
        LD0 = S["ld"].n
        g_loads = [
            (ident_s[:], dident),
            (gate_s[:].rearrange("p (a f) -> p a f", a=KT),
             dgate.rearrange("(a p) f -> p a f", p=P)),
            (inp_s[:].rearrange("p (a f) -> p a f", a=KT),
             dinp.rearrange("(a p) f -> p a f", p=P)),
            (outT_s[:], doutT),
        ]
        for dst, src in g_loads:
            pl.op("gpsimd", lambda e, d_=dst, s_=src: e.dma_start(out=d_, in_=s_),
                  inc=(S["gld"], 16))
        GLD0 = S["gld"].n

        pl.wait("scalar", S["ld"], LD0)
        pl.op("scalar", lambda e: e.activation(negA[:], Alog_s[:], AF.Exp),
              inc=(S["stat"], 1))
        pl.wait("vector", S["stat"], S["stat"].n)
        pl.op("vector", lambda e: e.tensor_scalar_mul(negA[:], negA[:], -1.0),
              inc=(S["stat"], 1))

        C = dict(ug=0, ud=0, gsl=0, usl=0, wsl=0, ne=0, tr=0)
        state = dict(stt_val=[], tile_last_stt={})

        def emit_transposes(src_tile_fn, nt_tiles, prep_fn, xw_entry, rcp_entry):
            base = S["tre"].n
            first = True
            for tt in range(nt_tiles):
                psem, pval = prep_fn(tt)
                pl.wait("tensor", psem, pval)
                if first:
                    pl.wait("tensor", S["xw"], xw_entry)
                    pl.wait("tensor", S["rcp"], rcp_entry)
                    first = False
                for kt in range(KT):
                    k = C["tr"]
                    sl = k % 4
                    if k >= 4:
                        pl.wait("tensor", S["tre"], k - 3)
                    pl.op("tensor", lambda e, s_=src_tile_fn(tt, kt), d_=tslots[sl]:
                          e.transpose(d_, s_, ident_s[:]), inc=(S["trp"], 1))
                    eng = "scalar" if (kt % 2 == 0) else "vector"
                    pl.wait(eng, S["trp"], S["trp"].n)
                    dst = lnT[:, kt * T + tt * P: kt * T + (tt + 1) * P]
                    if eng == "scalar":
                        pl.op(eng, lambda e, d_=dst, s_=tslots[sl]: e.copy(d_, s_),
                              inc=(S["tre"], 1))
                    else:
                        pl.op(eng, lambda e, d_=dst, s_=tslots[sl]:
                              e.tensor_copy(d_, s_), inc=(S["tre"], 1))
                    C["tr"] += 1
            return base

        def ffn_call(ph, xw_entry, rcp_entry, xw_tile=None):
            xw_base = S["xw"].n

            pl.wait("vector", S["ld"], 16)
            for tt in range(NT):
                pl.op("vector", lambda e, t=tt: e.reduce_sum(
                    fsum[:, t:t + 1], xr[:, t * D:(t + 1) * D], axis=AXX),
                    inc=(S["stat"], 1))
            pl.wait("scalar", S["ld"], 16)
            dump = lnT[:, 0:D]
            for tt in range(NT):
                if xw_tile is not None:
                    pl.wait("scalar", S["xw"], xw_tile(tt))
                pl.op("scalar", lambda e, t=tt: e.activation(
                    dump, xr[:, t * D:(t + 1) * D], AF.Square,
                    accum_out=fsq[:, t:t + 1]),
                    inc=(S["stat"], 1))
            pl.wait("vector", S["stat"], S["stat"].n)
            pl.op("vector", lambda e: e.tensor_scalar_mul(fmu[:], fsum[:], 1.0 / D))
            pl.op("vector", lambda e: e.tensor_scalar_mul(fvar[:], fsq[:], 1.0 / D))
            pl.raw("vector", lambda e: e.drain())
            pl.op("vector", lambda e: e.tensor_mul(ftmp[:], fmu[:], fmu[:]))
            pl.raw("vector", lambda e: e.drain())
            pl.op("vector", lambda e: e.tensor_sub(fvar[:], fvar[:], ftmp[:]))
            pl.raw("vector", lambda e: e.drain())
            pl.op("vector", lambda e: e.tensor_scalar_add(ftmp[:], fvar[:], EPS),
                  inc=(S["stat"], 1))
            pl.wait("scalar", S["stat"], S["stat"].n)
            pl.op("scalar", lambda e: e.activation(ftmp[:], ftmp[:], AF.Sqrt),
                  inc=(S["stat"], 1))
            pl.wait("vector", S["stat"], S["stat"].n)
            pl.op("vector", lambda e: e.reciprocal(frstd[:], ftmp[:]))
            pl.raw("vector", lambda e: e.drain())
            pl.op("vector", lambda e: e.tensor_mul(fnmu[:], fmu[:], frstd[:]))
            pl.raw("vector", lambda e: e.drain())
            pl.op("vector", lambda e: e.tensor_scalar_mul(fnmu[:], fnmu[:], -1.0),
                  inc=(S["stat"], 1))
            stats_done = S["stat"].n

            ln1tok = state["ln1tok"]

            def prep(tt):
                if tt == 0:
                    pl.wait("scalar", S["stat"], stats_done)
                if tt >= 1:
                    pl.wait("scalar", S["trp"], C["tr"])
                pl.op("scalar", lambda e, t=tt: e.activation(
                    ln1tok[:, 0:D],
                    xr[:, t * D:(t + 1) * D], AF.Identity,
                    bias=fnmu[:, t:t + 1], scale=frstd[:, t:t + 1]),
                    inc=(S["lntok"], 1))
                return S["lntok"], S["lntok"].n

            def srcf(tt, kt):
                return ln1tok[:, kt * P:(kt + 1) * P]

            tre_base = emit_transposes(srcf, NT, prep, xw_entry, rcp_entry)

            pl.wait("tensor", S["gld"], GLD0)
            pl.wait("tensor", S["tre"], tre_base + NT * KT)
            for tt in range(NT):
                for kt in range(KT):
                    pl.op("tensor", lambda e, t=tt, k=kt: e.matmul(
                        pD[:, t * 8:(t + 1) * 8],
                        lnT[:, k * T + t * P: k * T + (t + 1) * P],
                        gate_s[:, k * 8:(k + 1) * 8],
                        start=(k == 0), stop=(k == KT - 1)),
                        inc=(S["rt"], 1) if kt == KT - 1 else None)
            pl.wait("vector", S["rt"], S["rt"].n)
            pl.op("vector", lambda e: e.tensor_copy(lg_s[:], pD[:, 0:NT * 8]),
                  inc=(S["rcp"], 1))
            pl.raw("vector", lambda e: e.drain())
            lg3 = lg_s[:].rearrange("p (a f) -> p a f", a=NT)
            pl.op("vector", lambda e: e.reduce_max(m1_s[:], lg3, axis=AXX))
            pl.raw("vector", lambda e: e.drain())
            for tt in range(NT):
                pl.op("vector", lambda e, t=tt: e.tensor_scalar(
                    eq1_s[:, t * 8:(t + 1) * 8], lg_s[:, t * 8:(t + 1) * 8],
                    m1_s[:, t:t + 1], None, ALU.is_equal))
            pl.raw("vector", lambda e: e.drain())
            for tt in range(NT):
                pl.op("vector", lambda e, t=tt: e.scalar_tensor_tensor(
                    msk_s[:, t * 8:(t + 1) * 8], eq1_s[:, t * 8:(t + 1) * 8],
                    -1e30, lg_s[:, t * 8:(t + 1) * 8], ALU.mult, ALU.add))
            pl.raw("vector", lambda e: e.drain())
            msk3 = msk_s[:].rearrange("p (a f) -> p a f", a=NT)
            pl.op("vector", lambda e: e.reduce_max(m2_s[:], msk3, axis=AXX))
            pl.raw("vector", lambda e: e.drain())
            pl.op("vector", lambda e: e.tensor_sub(e2_s[:], m2_s[:], m1_s[:]),
                  inc=(S["stat"], 1))
            pl.wait("scalar", S["stat"], S["stat"].n)
            pl.op("scalar", lambda e: e.activation(e2_s[:], e2_s[:], AF.Exp),
                  inc=(S["stat"], 1))
            pl.wait("vector", S["stat"], S["stat"].n)
            pl.op("vector", lambda e: e.tensor_scalar_add(w1_s[:], e2_s[:], 1.0))
            pl.raw("vector", lambda e: e.drain())
            pl.op("vector", lambda e: e.reciprocal(w1_s[:], w1_s[:]))
            pl.raw("vector", lambda e: e.drain())
            pl.op("vector", lambda e: e.tensor_scalar_mul(w1_s[:], w1_s[:], 0.5))
            pl.raw("vector", lambda e: e.drain())
            pl.op("vector", lambda e: e.tensor_mul(w2_s[:], e2_s[:], w1_s[:]))
            pl.raw("vector", lambda e: e.drain())
            for tt in range(NT):
                pl.op("vector", lambda e, t=tt: e.tensor_scalar(
                    msk_s[:, t * 8:(t + 1) * 8], msk_s[:, t * 8:(t + 1) * 8],
                    m2_s[:, t:t + 1], None, ALU.is_equal))
            pl.raw("vector", lambda e: e.drain())
            for tt in range(NT):
                pl.op("vector", lambda e, t=tt: e.tensor_scalar(
                    msk_s[:, t * 8:(t + 1) * 8], msk_s[:, t * 8:(t + 1) * 8],
                    w2_s[:, t:t + 1], None, ALU.mult))
            pl.raw("vector", lambda e: e.drain())
            for tt in range(NT):
                pl.op("vector", lambda e, t=tt: e.scalar_tensor_tensor(
                    comb[:, t * 8:(t + 1) * 8], eq1_s[:, t * 8:(t + 1) * 8],
                    w1_s[:, t:t + 1], msk_s[:, t * 8:(t + 1) * 8],
                    ALU.mult, ALU.add))
            pl.raw("vector", lambda e: e.drain())

            h = state["h"]
            wd = state["wd"]
            def emit_slice(e_i, i):
                for (dsrc, buf, cnt, sld, slf) in (
                        (dwg, wg2, "gsl", "wg", "pg"),
                        (dwu, wu2, "usl", "wu", "pu")):
                    sl = C[cnt]
                    if sl >= 3:
                        pl.wait("gpsimd", S[slf], 2 * (sl - 3) + 2)
                    src = dsrc[e_i][:, i * P:(i + 1) * P].rearrange(
                        "(a p) i -> p a i", p=P)
                    dst = buf[:, (sl % 3) * KT * P:((sl % 3) + 1) * KT * P] \
                        .rearrange("p (a i) -> p a i", a=KT)
                    pl.op("gpsimd", lambda e, d_=dst, s_=src:
                          e.dma_start(out=d_, in_=s_), inc=(S[sld], 16))
                    C[cnt] += 1

            for e_i in range(n_experts):
                shared = (e_i == NE - 1)
                emit_slice(e_i, 0)
                emit_slice(e_i, 1)
                if C["ne"] >= 1:
                    pl.wait("gpsimd", S["pd"], 32 * C["ne"])
                if ph == 1 and e_i == 0:
                    pl.wait("gpsimd", S["xw"], xw_entry)
                for i in range(IT):
                    src = dwd[e_i][i * P:(i + 1) * P, :]
                    dst = wd[:, i * D:(i + 1) * D]
                    pl.op("gpsimd", lambda e, d_=dst, s_=src:
                          e.dma_start(out=d_, in_=s_), inc=(S["wd"], 16))
                    C["wsl"] += 1
                for i in range(2, IT):
                    emit_slice(e_i, i)

                for i in range(IT):
                    for tc in range(2):
                        ug = C["ug"]
                        gsl_idx = C["ne"] * IT + i
                        pl.wait("tensor", S["wg"], 16 * (gsl_idx + 1))
                        if ug >= 2:
                            pl.wait("tensor", S["sg"], ug - 1)
                        if i == 0 and e_i == 0:
                            pl.wait("tensor", S["tre"],
                                    tre_base + (4 if tc == 0 else 8) * KT)
                        gslot = pA[:, (ug % 2) * 512:((ug % 2) + 1) * 512]
                        for kt in range(KT):
                            pl.op("tensor", lambda e, k=kt, t=tc, sl=gslot,
                                  b=(gsl_idx % 3): e.matmul(
                                      sl,
                                      wg2[:, b * KT * P + k * P:
                                          b * KT * P + (k + 1) * P],
                                      lnT[:, k * T + t * 512:
                                          k * T + (t + 1) * 512],
                                      start=(k == 0), stop=(k == KT - 1)),
                                  inc=(S["pg"], 1) if kt == KT - 1 else None)
                        pl.wait("tensor", S["wu"], 16 * (gsl_idx + 1))
                        if ug >= 2:
                            pl.wait("tensor", S["h"], ug - 1)
                        uslot = pA[:, 1024 + (ug % 2) * 512:
                                   1024 + ((ug % 2) + 1) * 512]
                        for kt in range(KT):
                            pl.op("tensor", lambda e, k=kt, t=tc, sl=uslot,
                                  b=(gsl_idx % 3): e.matmul(
                                      sl,
                                      wu2[:, b * KT * P + k * P:
                                          b * KT * P + (k + 1) * P],
                                      lnT[:, k * T + t * 512:
                                          k * T + (t + 1) * 512],
                                      start=(k == 0), stop=(k == KT - 1)),
                                  inc=(S["pu"], 1) if kt == KT - 1 else None)
                        pl.wait("scalar", S["pg"], ug + 1)
                        if ug >= 2:
                            pl.wait("scalar", S["h"], ug - 1)
                        sgslot = sg2[:, (ug % 2) * 512:((ug % 2) + 1) * 512]
                        pl.op("scalar", lambda e, sl=sgslot, g=gslot:
                              e.activation(sl, g, AF.Silu), inc=(S["sg"], 1))
                        pl.wait("vector", S["pu"], ug + 1)
                        pl.wait("vector", S["sg"], ug + 1)
                        hdst = h[:, i * T + tc * 512: i * T + (tc + 1) * 512]
                        pl.op("vector", lambda e, d_=hdst, sl=sgslot, u=uslot:
                              e.tensor_mul(d_, sl, u), inc=(S["h"], 1))
                        C["ug"] += 1

                pl.wait("tensor", S["h"], C["ug"])
                pl.wait("tensor", S["wd"], 16 * C["wsl"])
                if e_i == 0:
                    pl.wait("tensor", S["tre"], tre_base + NT * KT)
                for t128 in range(NT):
                    for hh in range(4):
                        ud = C["ud"]
                        if ud >= 2:
                            pl.wait("tensor", S["xw"], state["stt_val"][ud - 2])
                        dslot = pB[:, (ud % 2) * 512:((ud % 2) + 1) * 512]
                        for i in range(IT):
                            pl.op("tensor", lambda e, t=t128, q=hh, ii=i, sl=dslot:
                                  e.matmul(
                                      sl,
                                      h[:, ii * T + t * P: ii * T + (t + 1) * P],
                                      wd[:, ii * D + q * 512:
                                         ii * D + (q + 1) * 512],
                                      start=(ii == 0), stop=(ii == IT - 1)),
                                  inc=(S["pd"], 1) if i == IT - 1 else None)
                        pl.wait("vector", S["pd"], S["pd"].n)
                        xv = xr[:, t128 * D + hh * 512: t128 * D + (hh + 1) * 512]
                        if shared:
                            v = pl.op("vector", lambda e, x=xv, sl=dslot:
                                      e.scalar_tensor_tensor(
                                          x, sl, 0.5, x, ALU.mult, ALU.add),
                                      inc=(S["xw"], 1))
                        else:
                            cc = comb[:, t128 * 8 + e_i: t128 * 8 + e_i + 1]
                            v = pl.op("vector", lambda e, x=xv, sl=dslot, c=cc:
                                      e.scalar_tensor_tensor(
                                          x, sl, c, x, ALU.mult, ALU.add),
                                      inc=(S["xw"], 1))
                        state["stt_val"].append(v)
                        if e_i == n_experts - 1 and hh == 3:
                            state["tile_last_stt"][(ph, t128)] = v
                        C["ud"] += 1
                C["ne"] += 1

        g_h1 = nc.sbuf_tensor("h1", [P, IT * T], BF16)
        g_tok1 = nc.sbuf_tensor("tok1", [P, D], BF16)
        g_wd1 = nc.sbuf_tensor("wd1", [P, IT * D], BF16)
        state["h"] = g_h1.__enter__()
        state["ln1tok"] = g_tok1.__enter__()
        state["wd"] = g_wd1.__enter__()
        ffn_call(0, xw_entry=0, rcp_entry=0)
        F1_XW = S["xw"].n
        g_wd1.__exit__(None, None, None)
        g_tok1.__exit__(None, None, None)
        g_h1.__exit__(None, None, None)
        mamba_xw = F1_XW

        if do_mamba:
            m_guards = []

            def msb(name, shape, dt):
                g = nc.sbuf_tensor(name, list(shape), dt)
                m_guards.append(g)
                return g.__enter__()

            zz = msb("zz", [P, D], BF16)
            wbc = msb("wbc", [P, D], BF16)
            bbc = msb("bbc", [P, D], BF16)
            xin = msb("xin", [DIN, T + 2], F32)
            u_s = msb("u_s", [DIN, T], F32)
            delta = msb("delta", [DIN, T], F32)
            dA0 = msb("dA0", [DIN, T], F32)
            dA1 = msb("dA1", [DIN, T], F32)
            dBu = msb("dBu", [DIN, T], F32)
            xsc = msb("xsc", [DIN, T], F32)
            y_s = msb("y_s", [DIN, T], F32)
            Bp_s = msb("Bp_s", [DST, T], F32)
            Cp_s = msb("Cp_s", [DST, T], F32)
            y16 = msb("y16", [DIN, T], BF16)
            bcB = msb("bcB", [DIN, 2 * T], F32)
            bcC = msb("bcC", [DIN, 2 * T], F32)
            dAbufs = [dA0, dA1]

            pl.wait("gpsimd", S["xw"], F1_XW)
            pl.op("gpsimd", lambda e: e.dma_start(out=wbc[:], in_=dlnw),
                  inc=(S["gld"], 16))
            pl.op("gpsimd", lambda e: e.dma_start(out=bbc[:], in_=dlnb),
                  inc=(S["gld"], 16))
            GLDW = S["gld"].n

            for tt in range(NT):
                pl.op("vector", lambda e, t=tt: e.reduce_sum(
                    fsum[:, t:t + 1], xr[:, t * D:(t + 1) * D], axis=AXX),
                    inc=(S["mA"], 1))
            pl.wait("scalar", S["pu"], S["pu"].n)
            for tt in range(NT):
                pl.wait("scalar", S["xw"], state["tile_last_stt"][(0, tt)])
                pl.op("scalar", lambda e, t=tt: e.activation(
                    lnT[:, 0:D], xr[:, t * D:(t + 1) * D], AF.Square,
                    accum_out=fsq[:, t:t + 1]),
                    inc=(S["mA"], 1))
            pl.wait("vector", S["mA"], S["mA"].n)
            pl.op("vector", lambda e: e.tensor_scalar_mul(mmu[:], fsum[:], 1.0 / D))
            pl.op("vector", lambda e: e.tensor_scalar_mul(mvar[:], fsq[:], 1.0 / D))
            pl.raw("vector", lambda e: e.drain())
            pl.op("vector", lambda e: e.tensor_mul(mtmp[:], mmu[:], mmu[:]))
            pl.raw("vector", lambda e: e.drain())
            pl.op("vector", lambda e: e.tensor_sub(mvar[:], mvar[:], mtmp[:]))
            pl.raw("vector", lambda e: e.drain())
            pl.op("vector", lambda e: e.tensor_scalar_add(mtmp[:], mvar[:], EPS),
                  inc=(S["mB"], 1))
            pl.wait("scalar", S["mB"], S["mB"].n)
            pl.op("scalar", lambda e: e.activation(mtmp[:], mtmp[:], AF.Sqrt),
                  inc=(S["mB"], 1))
            pl.wait("vector", S["mB"], S["mB"].n)
            pl.op("vector", lambda e: e.reciprocal(mrstd[:], mtmp[:]))
            pl.raw("vector", lambda e: e.drain())
            pl.op("vector", lambda e: e.tensor_mul(mtmp[:], mvar[:], mrstd[:]))
            pl.raw("vector", lambda e: e.drain())
            pl.op("vector", lambda e: e.tensor_mul(mtmp[:], mtmp[:], mrstd[:]))
            pl.raw("vector", lambda e: e.drain())
            pl.op("vector", lambda e: e.tensor_scalar_add(mtmp[:], mtmp[:], EPS),
                  inc=(S["mB"], 1))
            pl.wait("scalar", S["mB"], S["mB"].n)
            pl.op("scalar", lambda e: e.activation(mtmp[:], mtmp[:], AF.Sqrt),
                  inc=(S["mB"], 1))
            pl.wait("vector", S["mB"], S["mB"].n)
            pl.op("vector", lambda e: e.reciprocal(mtmp[:], mtmp[:]))
            pl.raw("vector", lambda e: e.drain())
            pl.op("vector", lambda e: e.tensor_mul(mseff[:], mrstd[:], mtmp[:]))
            pl.raw("vector", lambda e: e.drain())
            pl.op("vector", lambda e: e.tensor_mul(mnb[:], mmu[:], mseff[:]))
            pl.op("vector", lambda e: e.tensor_mul(mnmr[:], mmu[:], mrstd[:]))
            pl.raw("vector", lambda e: e.drain())
            pl.op("vector", lambda e: e.tensor_scalar_mul(mnb[:], mnb[:], -1.0))
            pl.op("vector", lambda e: e.tensor_scalar_mul(mnmr[:], mnmr[:], -1.0),
                  inc=(S["mB"], 1))
            mstats = S["mB"].n

            def mprep(tt):
                if tt == 0:
                    pl.wait("scalar", S["mB"], mstats)
                    pl.wait("scalar", S["gld"], GLDW)
                    pl.wait("scalar", S["xw"], F1_XW)
                if tt >= 1:
                    pl.wait("scalar", S["trp"], C["tr"])
                pl.op("scalar", lambda e, t=tt: e.activation(
                    zz[:, 0:D],
                    xr[:, t * D:(t + 1) * D], AF.Identity,
                    bias=mnb[:, t:t + 1], scale=mseff[:, t:t + 1]),
                    inc=(S["mC"], 1))
                pl.wait("vector", S["mC"], S["mC"].n)
                zv = zz[:, 0:D]
                pl.op("vector", lambda e, z=zv: e.tensor_mul(z, z, wbc[:]))
                pl.op("vector", lambda e, z=zv: e.tensor_add(z, z, bbc[:]),
                      inc=(S["mC"], 1))
                return S["mC"], S["mC"].n

            def msrc(tt, kt):
                return zz[:, kt * P:(kt + 1) * P]

            tre_base2 = emit_transposes(msrc, NT, mprep, F1_XW, 1)

            pl.wait("tensor", S["tre"], tre_base2 + NT * KT)
            pl.wait("tensor", S["gld"], GLD0)
            for tc in range(2):
                for kt in range(KT):
                    pl.op("tensor", lambda e, k=kt, t=tc: e.matmul(
                        pA[0:DIN, t * 512:(t + 1) * 512],
                        inp_s[:, k * DIN:(k + 1) * DIN],
                        lnT[:, k * T + t * 512: k * T + (t + 1) * 512],
                        start=(k == 0), stop=(k == KT - 1)),
                        inc=(S["mD"], 1) if (kt == KT - 1) else None)
            pl.wait("scalar", S["mD"], S["mD"].n)
            pl.op("scalar", lambda e: e.copy(xin[:, 1:T + 1], pA[0:DIN, 0:T]),
                  inc=(S["mD"], 1))
            inproj_done = S["mD"].n

            pl.wait("sync", S["mD"], inproj_done)
            pl.op("sync", lambda e: e.dma_start(
                out=dag1i[:, 0:1], in_=xin[:, 1:2],
                allow_slow_non_contiguous=True), inc=(S["ag"], 16))
            pl.op("sync", lambda e: e.dma_start(
                out=dag1i[:, 1:2], in_=xin[:, T:T + 1],
                allow_slow_non_contiguous=True), inc=(S["ag"], 16))
            pl.wait("gpsimd", S["ag"], S["ag"].n)
            pl.op("gpsimd", lambda e: e.collective_compute(
                "AllGather", ALU.bypass, replica_groups=groups,
                ins=[dag1i], outs=[dag1o]), inc=(S["ag"], 1))
            pl.wait("gpsimd", S["ag"], S["ag"].n)
            pl.op("gpsimd", lambda e: e.dma_start(
                out=slots1[:].rearrange("p (r f) -> p r f", r=4),
                in_=dag1o.rearrange("(r p) f -> p r f", r=4)),
                inc=(S["ag"], 16))
            pl.wait("vector", S["ag"], S["ag"].n)
            pl.op("vector", lambda e: e.memset(xin[:, 0:1], 0.0))
            pl.op("vector", lambda e: e.memset(xin[:, T + 1:T + 2], 0.0))
            pl.raw("vector", lambda e: e.drain())
            sl1 = slots1[:].rearrange("p (r f) -> p r f", r=4)
            for r in range(4):
                pl.op("vector", lambda e, rr=r: e.scalar_tensor_tensor(
                    xin[:, 0:1], sl1[:, rr, 1:2], selL_s[:, rr:rr + 1],
                    xin[:, 0:1], ALU.mult, ALU.add))
                pl.raw("vector", lambda e: e.drain())
                pl.op("vector", lambda e, rr=r: e.scalar_tensor_tensor(
                    xin[:, T + 1:T + 2], sl1[:, rr, 0:1], selR_s[:, rr:rr + 1],
                    xin[:, T + 1:T + 2], ALU.mult, ALU.add))
                pl.raw("vector", lambda e: e.drain())

            pl.op("vector", lambda e: e.tensor_scalar(
                delta[:], xin[:, 0:T], convw_s[:, 0:1], None, ALU.mult))
            pl.op("vector", lambda e: e.scalar_tensor_tensor(
                delta[:], xin[:, 1:T + 1], convw_s[:, 1:2], delta[:],
                ALU.mult, ALU.add))
            pl.op("vector", lambda e: e.scalar_tensor_tensor(
                delta[:], xin[:, 2:T + 2], convw_s[:, 2:3], delta[:],
                ALU.mult, ALU.add), inc=(S["mD"], 1))
            pl.wait("scalar", S["mD"], S["mD"].n)
            pl.op("scalar", lambda e: e.activation(
                u_s[:], delta[:], AF.Silu, bias=convb_s[:]), inc=(S["mD"], 1))
            u_done = S["mD"].n

            pl.wait("tensor", S["mD"], u_done)
            for tc in range(2):
                pl.op("tensor", lambda e, t=tc: e.matmul(
                    pA[0:DST, 1024 + t * 512:1024 + (t + 1) * 512],
                    xpB_s[:], u_s[:, t * 512:(t + 1) * 512],
                    start=True, stop=True),
                    inc=(S["mE"], 1) if tc == 1 else None)
                pl.op("tensor", lambda e, t=tc: e.matmul(
                    pB[0:DST, t * 512:(t + 1) * 512],
                    xpC_s[:], u_s[:, t * 512:(t + 1) * 512],
                    start=True, stop=True),
                    inc=(S["mE"], 1) if tc == 1 else None)
            pl.wait("scalar", S["mE"], S["mE"].n)
            pl.op("scalar", lambda e: e.copy(Bp_s[:], pA[0:DST, 1024:2048]),
                  inc=(S["mE"], 1))
            pl.op("scalar", lambda e: e.copy(Cp_s[:], pB[0:DST, 0:T]),
                  inc=(S["mE"], 1))
            bc_evac = S["mE"].n
            pl.wait("tensor", S["mE"], bc_evac)
            for tc in range(2):
                pl.op("tensor", lambda e, t=tc: e.matmul(
                    pA[0:DTR, 1024 + t * 512:1024 + (t + 1) * 512],
                    xpd_s[:], u_s[:, t * 512:(t + 1) * 512],
                    start=True, stop=True),
                    inc=(S["mE"], 1) if tc == 1 else None)
            pl.wait("scalar", S["mE"], S["mE"].n)
            pl.op("scalar", lambda e: e.copy(xsc[0:DTR, :], pA[0:DTR, 1024:2048]),
                  inc=(S["mE"], 1))
            dr_evac = S["mE"].n
            pl.wait("tensor", S["mE"], dr_evac)
            for tc in range(2):
                pl.op("tensor", lambda e, t=tc: e.matmul(
                    pA[0:DIN, t * 512:(t + 1) * 512],
                    dtT_s[:], xsc[0:DTR, t * 512:(t + 1) * 512],
                    start=True, stop=True),
                    inc=(S["mE"], 1) if tc == 1 else None)
            pl.wait("scalar", S["mE"], S["mE"].n)
            pl.op("scalar", lambda e: e.activation(
                dA0[:], pA[0:DIN, 0:T], AF.Abs, bias=dtb_s[:]))
            pl.op("scalar", lambda e: e.activation(
                dA1[:], dA0[:], AF.Exp, scale=-1.0))
            pl.op("scalar", lambda e: e.activation(
                dA1[:], dA1[:], AF.Ln, bias=1.0))
            pl.op("scalar", lambda e: e.activation(
                dA0[:], pA[0:DIN, 0:T], AF.Relu, bias=dtb_s[:]),
                inc=(S["mE"], 1))
            pl.wait("vector", S["mE"], S["mE"].n)
            pl.op("vector", lambda e: e.tensor_add(delta[:], dA0[:], dA1[:]))
            pl.op("vector", lambda e: e.tensor_scalar_mul(delta[:], delta[:],
                                                          0.001))
            pl.raw("vector", lambda e: e.drain())
            pl.op("vector", lambda e: e.reduce_sum(Stot_s[:], delta[:], axis=AXX))
            pl.raw("vector", lambda e: e.drain())
            pl.op("vector", lambda e: e.tensor_scalar(
                Atot_s[:], negA[:], Stot_s[:], None, ALU.mult), inc=(S["mF"], 1))
            pl.wait("scalar", S["mF"], S["mF"].n)
            pl.op("scalar", lambda e: e.activation(Atot_s[:], Atot_s[:], AF.Exp),
                  inc=(S["mF"], 1))
            prep_done = S["mF"].n

            pl.wait("sync", S["mE"], bc_evac)
            pl.op("sync", lambda e: e.dma_start(out=dbpcp[0:DST, :], in_=Bp_s[:]),
                  inc=(S["bcl"], 16))
            pl.op("sync", lambda e: e.dma_start(out=dbpcp[DST:2 * DST, :],
                                                in_=Cp_s[:]),
                  inc=(S["bcl"], 16))

            nbc = 0
            for n in range(DST):
                if nbc >= 2:
                    pl.wait("sync", S["bcf"], nbc - 1)
                bslot = bcB[:, (n % 2) * T:((n % 2) + 1) * T]
                pl.op("sync", lambda e, d_=bslot, r=n: e.dma_start(
                    out=d_, in_=dbpcp[r:r + 1, :].broadcast_to([DIN, T])),
                    inc=(S["bcl"], 16))
                nbc += 1
                if n >= 2:
                    pl.wait("scalar", S["mG"], 3 * (n - 2) + 2)
                pl.op("scalar", lambda e, r=n, b=dAbufs[n % 2]: e.activation(
                    b[:], delta[:], AF.Exp, scale=negA[:, r:r + 1]),
                    inc=(S["mF"], 1))
                if n >= 1:
                    pl.wait("gpsimd", S["mG"], 3 * (n - 1) + 2)
                else:
                    pl.wait("gpsimd", S["mF"], prep_done)
                pl.op("gpsimd", lambda e: e.tensor_mul(dBu[:], delta[:], u_s[:]),
                      inc=(S["mY"], 1))
                pl.wait("vector", S["mF"], prep_done + n + 1)
                pl.wait("vector", S["mY"], S["mY"].n)
                pl.wait("vector", S["bcl"], 32 + 16 * nbc)
                pl.op("vector", lambda e, b=bslot: e.tensor_mul(dBu[:], dBu[:], b),
                      inc=(S["bcf"], 1), inc2=(S["mG"], 1))
                pl.op("vector", lambda e, b=dAbufs[n % 2]: e.tensor_tensor_scan(
                    xsc[:], b[:], dBu[:], 0.0, ALU.mult, ALU.add),
                    inc=(S["mG"], 1))
                pl.raw("vector", lambda e: e.drain())
                pl.op("vector", lambda e, r=n: e.tensor_copy(
                    F_s[:, r:r + 1], xsc[:, T - 1:T]), inc=(S["mG"], 1))
                pl.raw("vector", lambda e: e.drain())
            pass1_done = S["mG"].n

            pl.wait("sync", S["mG"], pass1_done)
            pl.op("sync", lambda e: e.dma_start(out=dag2i[:, 0:DST], in_=F_s[:]),
                  inc=(S["ag"], 16))
            pl.op("sync", lambda e: e.dma_start(out=dag2i[:, DST:2 * DST],
                                                in_=Atot_s[:]),
                  inc=(S["ag"], 16))
            pl.wait("gpsimd", S["ag"], S["ag"].n)
            pl.op("gpsimd", lambda e: e.collective_compute(
                "AllGather", ALU.bypass, replica_groups=groups,
                ins=[dag2i], outs=[dag2o]), inc=(S["ag"], 1))
            pl.wait("gpsimd", S["ag"], S["ag"].n)
            pl.op("gpsimd", lambda e: e.dma_start(
                out=slots2[:].rearrange("p (r f) -> p r f", r=4),
                in_=dag2o.rearrange("(r p) f -> p r f", r=4)),
                inc=(S["ag"], 16))
            pl.wait("vector", S["ag"], S["ag"].n)
            sl2 = slots2[:].rearrange("p (r f) -> p r f", r=4)
            pl.op("vector", lambda e: e.tensor_copy(cs1[:], sl2[:, 0, 0:DST]))
            pl.raw("vector", lambda e: e.drain())
            pl.op("vector", lambda e: e.tensor_mul(ctmp[:],
                                                   sl2[:, 1, DST:2 * DST], cs1[:]))
            pl.raw("vector", lambda e: e.drain())
            pl.op("vector", lambda e: e.tensor_add(cs2[:], ctmp[:],
                                                   sl2[:, 1, 0:DST]))
            pl.raw("vector", lambda e: e.drain())
            pl.op("vector", lambda e: e.tensor_mul(ctmp[:],
                                                   sl2[:, 2, DST:2 * DST], cs2[:]))
            pl.raw("vector", lambda e: e.drain())
            pl.op("vector", lambda e: e.tensor_add(cs3[:], ctmp[:],
                                                   sl2[:, 2, 0:DST]))
            pl.raw("vector", lambda e: e.drain())
            pl.op("vector", lambda e: e.memset(carry[:], 0.0))
            pl.raw("vector", lambda e: e.drain())
            for j, csj in enumerate((cs1, cs2, cs3)):
                pl.op("vector", lambda e, c=csj, jj=j: e.scalar_tensor_tensor(
                    carry[:], c[:], selc_s[:, jj:jj + 1], carry[:],
                    ALU.mult, ALU.add))
                pl.raw("vector", lambda e: e.drain())
            pl.op("vector", lambda e: e.memset(y_s[:], 0.0), inc=(S["mG"], 1))

            for n in range(DST):
                if nbc >= 2:
                    pl.wait("sync", S["bcf"], nbc - 1)
                bslot = bcB[:, (n % 2) * T:((n % 2) + 1) * T]
                pl.op("sync", lambda e, d_=bslot, r=n: e.dma_start(
                    out=d_, in_=dbpcp[r:r + 1, :].broadcast_to([DIN, T])),
                    inc=(S["bcl"], 16))
                nbc += 1
                if nbc >= 2:
                    pl.wait("sync", S["bcf"], nbc - 1)
                cslot = bcC[:, (n % 2) * T:((n % 2) + 1) * T]
                pl.op("sync", lambda e, d_=cslot, r=n: e.dma_start(
                    out=d_,
                    in_=dbpcp[DST + r:DST + r + 1, :].broadcast_to([DIN, T])),
                    inc=(S["bcl"], 16))
                nbc += 1
                if n >= 2:
                    pl.wait("scalar", S["mH"], 3 * (n - 2) + 2)
                pl.op("scalar", lambda e, r=n, b=dAbufs[n % 2]: e.activation(
                    b[:], delta[:], AF.Exp, scale=negA[:, r:r + 1]),
                    inc=(S["mF"], 1))
                if n >= 1:
                    pl.wait("gpsimd", S["mH"], 3 * (n - 1) + 2)
                else:
                    pl.wait("gpsimd", S["mG"], S["mG"].n)
                pl.op("gpsimd", lambda e: e.tensor_mul(dBu[:], delta[:], u_s[:]),
                      inc=(S["mY"], 1))
                pl.wait("vector", S["mF"], prep_done + DST + n + 1)
                pl.wait("vector", S["mY"], S["mY"].n)
                pl.wait("vector", S["bcl"], 32 + 16 * nbc)
                pl.op("vector", lambda e, b=bslot: e.tensor_mul(dBu[:], dBu[:], b),
                      inc=(S["bcf"], 1), inc2=(S["mH"], 1))
                pl.op("vector", lambda e, b=dAbufs[n % 2], r=n:
                      e.tensor_tensor_scan(
                          xsc[:], b[:], dBu[:], carry[:, r:r + 1],
                          ALU.mult, ALU.add),
                      inc=(S["mH"], 1))
                pl.op("vector", lambda e, c=cslot: e.tensor_mul(dBu[:], xsc[:], c),
                      inc=(S["mH"], 1), inc2=(S["bcf"], 1))
                pl.wait("gpsimd", S["mH"], 3 * n + 3)
                pl.op("gpsimd", lambda e: e.tensor_add(y_s[:], y_s[:], dBu[:]),
                      inc=(S["mY"], 1))
            pl.wait("vector", S["mY"], S["mY"].n)
            pl.op("vector", lambda e: e.scalar_tensor_tensor(
                y_s[:], u_s[:], Dp_s[:], y_s[:], ALU.mult, ALU.add))
            pl.op("vector", lambda e: e.tensor_copy(y16[:], y_s[:]),
                  inc=(S["mH"], 1))
            y_done = S["mH"].n

            pl.wait("tensor", S["mH"], y_done)
            for tt in range(NT):
                for q in range(4):
                    pl.op("tensor", lambda e, t=tt, qq=q: e.matmul(
                        pA[:, qq * 512:(qq + 1) * 512],
                        y16[:, t * P:(t + 1) * P],
                        outT_s[:, qq * 512:(qq + 1) * 512],
                        start=True, stop=True),
                        inc=(S["mH"], 1) if q == 3 else None)
                pl.wait("vector", S["mH"], S["mH"].n)
                xv = xr[:, tt * D:(tt + 1) * D]
                pl.op("vector", lambda e, x=xv, t=tt: e.scalar_tensor_tensor(
                    x, x, mrstd[:, t:t + 1], x, ALU.mult, ALU.add))
                pl.op("vector", lambda e, x=xv, t=tt: e.tensor_scalar(
                    x, x, mnmr[:, t:t + 1], None, ALU.add))
                vres = pl.op("vector", lambda e, x=xv: e.tensor_add(
                    x, x, pA[:, 0:D]), inc=(S["xw"], 1))
                pl.wait("tensor", S["xw"], vres)
            for g in reversed(m_guards):
                g.__exit__(None, None, None)
            mamba_xw = S["xw"].n

        if do_ffn2:
            g_h2 = nc.sbuf_tensor("h2", [P, IT * T], BF16)
            g_tok2 = nc.sbuf_tensor("tok2", [P, D], BF16)
            g_wd2 = nc.sbuf_tensor("wd2", [P, IT * D], BF16)
            state["h"] = g_h2.__enter__()
            state["ln1tok"] = g_tok2.__enter__()
            state["wd"] = g_wd2.__enter__()
            ffn_call(1, xw_entry=mamba_xw, rcp_entry=1,
                     xw_tile=(lambda tt, base=mamba_xw: base - NT + tt + 1)
                     if do_mamba else None)
            g_wd2.__exit__(None, None, None)
            g_tok2.__exit__(None, None, None)
            g_h2.__exit__(None, None, None)

        if do_ffn2:
            for tt in range(NT):
                pl.wait("sync", S["xw"], state["tile_last_stt"][(1, tt)])
                pl.op("sync", lambda e, t=tt: e.dma_start(
                    out=dy[t * P:(t + 1) * P, :], in_=xr[:, t * D:(t + 1) * D]),
                    inc=(S["st"], 16))
        else:
            pl.wait("sync", S["xw"], S["xw"].n)
            for tt in range(NT):
                pl.op("sync", lambda e, t=tt: e.dma_start(
                    out=dy[t * P:(t + 1) * P, :], in_=xr[:, t * D:(t + 1) * D]),
                    inc=(S["st"], 16))
        pl.wait("sync", S["st"], S["st"].n)

        with nc.Block() as block:
            block.tensor(lambda e: [f(e) for f in pl.q["tensor"]])
            block.vector(lambda e: [f(e) for f in pl.q["vector"]])
            block.scalar(lambda e: [f(e) for f in pl.q["scalar"]])
            block.sync(lambda e: [f(e) for f in pl.q["sync"]])
            block.gpsimd(lambda e: [f(e) for f in pl.q["gpsimd"]])

    return nc


def prep_inputs(inputs):
    f = np.ascontiguousarray
    w = {}
    eg = inputs["L0_ffn_expert_gate_w"]
    eu = inputs["L0_ffn_expert_up_w"]
    ed = inputs["L0_ffn_expert_down_w"]
    sg = inputs["L0_ffn_shared_gate_w"]
    su = inputs["L0_ffn_shared_up_w"]
    sd = inputs["L0_ffn_shared_down_w"]
    w["wgT"] = f(np.concatenate([eg.transpose(0, 2, 1), sg.T[None]], axis=0))
    w["wuT"] = f(np.concatenate([eu.transpose(0, 2, 1), su.T[None]], axis=0))
    w["wdT"] = f(np.concatenate([ed.transpose(0, 2, 1), sd.T[None]], axis=0))
    w["gateT"] = f(inputs["L0_ffn_gate_weight"].T)
    w["inpT"] = f(inputs["L0_attn_in_proj_W"].T)
    xp = inputs["L0_attn_x_proj_W"]
    w["xpd"] = f(xp[0:DTR].T)
    w["xpB"] = f(xp[DTR:DTR + DST].T)
    w["xpC"] = f(xp[DTR + DST:DTR + 2 * DST].T)
    w["dtT"] = f(inputs["L0_attn_dt_proj_W"].T)
    w["dtb"] = f(inputs["L0_attn_dt_proj_b"].reshape(DIN, 1))
    w["convw"] = f(inputs["L0_attn_conv_w"].reshape(DIN, 3))
    w["convb"] = f(inputs["L0_attn_conv_b"].reshape(DIN, 1))
    w["Alog"] = f(inputs["L0_attn_A_log"])
    w["Dp"] = f(inputs["L0_attn_D_param"].reshape(DIN, 1))
    w["outT"] = f(inputs["L0_attn_out_proj_W"].T)
    w["lnw_bc"] = f(np.broadcast_to(np.asarray(inputs["L0_attn_ln_w"])[None, :],
                                    (P, D)))
    w["lnb_bc"] = f(np.broadcast_to(np.asarray(inputs["L0_attn_ln_b"])[None, :],
                                    (P, D)))
    w["ident"] = np.eye(P, dtype=np.float32)
    for k in w:
        w[k] = np.asarray(w[k], dtype=np.float32)
    return w


def make_in_maps(x, w):
    in_maps = []
    for c in range(NCORES):
        b, ch = c // 4, c % 4
        m = dict(w)
        m["x"] = np.ascontiguousarray(x[b, ch * T:(ch + 1) * T, :])
        selc = np.zeros((DIN, 3), np.float32)
        if ch >= 1:
            selc[:, ch - 1] = 1.0
        selL = np.zeros((DIN, 4), np.float32)
        selR = np.zeros((DIN, 4), np.float32)
        if ch > 0:
            selL[:, ch - 1] = 1.0
        if ch < 3:
            selR[:, ch + 1] = 1.0
        m["selc"] = selc
        m["selL"] = selL
        m["selR"] = selR
        in_maps.append(m)
    return in_maps


def kernel(**inputs):
    x = np.asarray(inputs["L0_attn_feat_patch"], dtype=np.float32)
    w = prep_inputs(inputs)
    nc = build_program()
    in_maps = make_in_maps(x, w)
    res = run_bass_kernel_spmd(nc, in_maps, core_ids=list(range(NCORES)))
    out = np.empty((2, 4096, D), dtype=np.float32)
    for c in range(NCORES):
        b, ch = c // 4, c % 4
        out[b, ch * T:(ch + 1) * T, :] = np.asarray(res.results[c]["y"],
                                                    dtype=np.float32)
    return out


if __name__ == "__main__":
    import reference as _r
    inp = _r.setup_inputs()
    got = kernel(**{k: np.asarray(v) for k, v in inp.items()})
    want = np.asarray(_r.reference(**inp))
    err = np.linalg.norm(got - want) / np.linalg.norm(want)
    print("rel_err", err)


# version 12
# speedup vs baseline: 1.3296x; 1.3296x over previous
import contextlib
import numpy as np
import concourse.bass as bass
import concourse.mybir as mybir
from concourse.bass_utils import run_bass_kernel_spmd
from concourse.bass import IndirectOffsetOnAxis

F32 = mybir.dt.float32
BF16 = mybir.dt.bfloat16
AF = mybir.ActivationFunctionType
ALU = mybir.AluOpType
AXX = mybir.AxisListType.X

P = 128
T = 1024
NT = 8
D = 2048
KT = 16
II = 1408
IT = 11
NE = 9
DIN = 96
DST = 16
DTR = 6
CP = 384
CPP = 512
QS = [(0, 128), (128, 128), (256, 128)]
EPS = 1e-5
NCORES = 8

ENGS = ("tensor", "vector", "scalar", "sync", "gpsimd")


class Sem:
    def __init__(self, es, nc, name):
        self.h = es.enter_context(nc.semaphore(name))
        self.n = 0

    def bump(self, k=1):
        self.n += k
        return self.n


class Plan:
    def __init__(self):
        self.q = {k: [] for k in ENGS}

    def wait(self, eng, sem, val):
        if val <= 0:
            return
        self.q[eng].append(lambda e, s=sem.h, v=val: e.wait_ge(s, v))

    def wait_lazy(self, eng, sem, holder, key):
        self.q[eng].append(lambda e, s=sem.h: e.wait_ge(s, holder[key]))

    def op(self, eng, build, inc=None, inc2=None):
        if inc is None:
            self.q[eng].append(build)
            return None
        sem, k = inc
        v = sem.bump(k)
        self.q[eng].append(lambda e, b=build, s=sem.h, kk=k: b(e).then_inc(s, kk))
        if inc2 is not None:
            sem2, k2 = inc2
            sem2.bump(k2)
            self.q[eng].append(
                lambda e, s2=sem2.h, kk2=k2: e.nop().then_inc(s2, kk2))
        return v

    def raw(self, eng, fn):
        self.q[eng].append(fn)


def build_program(do_mamba=True, do_ffn2=True, n_experts=NE, ilv=True):
    nc = bass.Bass("TRN2", target_bir_lowering=False)

    def din(name, shape):
        return nc.dram_tensor(name, list(shape), F32, kind="ExternalInput").ap()

    dx = din("x", [T, D])
    dwg = din("wgT", [NE, D, II])
    dwu = din("wuT", [NE, D, II])
    dwd = din("wdT", [NE, II, D])
    dgate = din("gateT", [D, 8])
    dinp = din("inpT", [D, DIN])
    dxpB = din("xpB", [DIN, DST])
    dxpC = din("xpC", [DIN, DST])
    dxpd = din("xpd", [DIN, DTR])
    ddtT = din("dtT", [DTR, DIN])
    ddtb = din("dtb", [DIN, 1])
    dconvw = din("convw", [DIN, 3])
    dconvb = din("convb", [DIN, 1])
    dAlog = din("Alog", [DIN, DST])
    dDp = din("Dp", [DIN, 1])
    doutT = din("outT", [DIN, D])
    dlnw = din("lnw_bc", [P, D])
    dlnb = din("lnb_bc", [P, D])
    dident = din("ident", [P, P])
    dselc = din("selc", [DIN, 3])
    dselL = din("selL", [DIN, 4])
    dselR = din("selR", [DIN, 4])
    deiota = din("eiota", [P, NT * 8])
    dtokid = din("tokid", [P, NT])
    dy = nc.dram_tensor("y", [T, D], F32, kind="ExternalOutput").ap()
    dag1i = nc.dram_tensor("ag1i", [DIN, 2], F32, kind="Internal").ap()
    dag1o = nc.dram_tensor("ag1o", [4 * DIN, 2], F32, kind="Internal").ap()
    dag2i = nc.dram_tensor("ag2i", [DIN, 2 * DST], F32, kind="Internal").ap()
    dag2o = nc.dram_tensor("ag2o", [4 * DIN, 2 * DST], F32, kind="Internal").ap()
    dbpcp = nc.dram_tensor("bpcp", [2 * DST, T], F32, kind="Internal").ap()
    dln = nc.dram_tensor("dln", [T, D], BF16, kind="Internal").ap()
    dids = nc.dram_tensor("dids", [8 * CPP, 1], F32, kind="Internal").ap()
    dpo = nc.dram_tensor("dpo", [8 * CPP, D], BF16, kind="Internal").ap()
    groups = [[0, 1, 2, 3], [4, 5, 6, 7]]

    pl = Plan()
    es = contextlib.ExitStack()
    with es:
        sb = lambda name, shape, dt: es.enter_context(
            nc.sbuf_tensor(name, list(shape), dt))
        xr = sb("xr", [P, NT * D], F32)
        lnT = sb("lnT", [P, KT * T], BF16)
        wg2 = sb("wg2", [P, 3 * KT * P], BF16)
        wu2 = sb("wu2", [P, 3 * KT * P], BF16)
        ident_s = sb("ident_s", [P, P], BF16)
        gate_s = sb("gate_s", [P, KT * 8], BF16)
        sg2 = sb("sg2", [P, 2 * 512], F32)
        lg_s = sb("lg_s", [P, NT * 8], F32)
        m1_s = sb("m1_s", [P, NT], F32)
        m2_s = sb("m2_s", [P, NT], F32)
        e2_s = sb("e2_s", [P, NT], F32)
        w1_s = sb("w1_s", [P, NT], F32)
        w2_s = sb("w2_s", [P, NT], F32)
        eq1_s = sb("eq1_s", [P, NT * 8], F32)
        eq2_s = sb("eq2_s", [P, NT * 8], F32)
        msk_s = sb("msk_s", [P, NT * 8], F32)
        eiota_s = sb("eiota_s", [P, NT * 8], F32)
        ident32_s = sb("ident32_s", [P, P], F32)
        tokid_s = sb("tokid_s", [P, NT], F32)
        rix1 = sb("rix1", [P, NT], F32)
        rix2 = sb("rix2", [P, NT], F32)
        ix1 = sb("ix1", [P, NT], mybir.dt.int32)
        ix2 = sb("ix2", [P, NT], mybir.dt.int32)
        ids_sb = sb("ids_sb", [P, 8 * CPP // P], F32)
        ids_raw = sb("ids_raw", [8 * CPP // P, P], F32)
        idi_sb = sb("idi_sb", [P, 8 * CPP // P], mybir.dt.int32)
        fsum = sb("fsum", [P, NT], F32)
        fsq = sb("fsq", [P, NT], F32)
        fmu = sb("fmu", [P, NT], F32)
        fvar = sb("fvar", [P, NT], F32)
        ftmp = sb("ftmp", [P, NT], F32)
        frstd = sb("frstd", [P, NT], F32)
        fnmu = sb("fnmu", [P, NT], F32)

        pA = es.enter_context(nc.psum_tensor("pA", [P, 2048], F32))
        pB = es.enter_context(nc.psum_tensor("pB", [P, 1024], F32))
        pC = es.enter_context(nc.psum_tensor("pC", [P, 512], F32))
        pD = es.enter_context(nc.psum_tensor("pD", [P, 512], F32))
        pB16 = pB[:].bitcast(BF16)
        pC16 = pC[:].bitcast(BF16)
        pD16 = pD[:].bitcast(BF16)
        tslots = [pB16[:, 0:P], pB16[:, 1024:1024 + P],
                  pC16[:, 0:P], pD16[:, 2 * P:3 * P]]
        ptslots = [pC16[:, 0:P], pD16[:, 2 * P:3 * P]]

        S = {}
        for name in ("ld", "gld", "wg", "wgf", "wu", "wuf", "wd", "wdf",
                     "pg", "pu", "sg", "h", "pd", "xw", "trp", "tre",
                     "lntok", "rt", "rcp", "stat",
                     "mA", "mB", "mC", "mD", "mE", "mF", "mG", "mH",
                     "bcl", "bcf", "ag", "st", "mY",
                     "mtr", "mte", "lns", "ids", "pk", "pk2", "pv", "ps",
                     "ptr", "pte"):
            S[name] = Sem(es, nc, "s_" + name)

        sync_loads = [
            (xr[:].rearrange("p (a f) -> p a f", a=NT),
             dx.rearrange("(a p) f -> p a f", p=P)),
        ] + [
            (eiota_s[:], deiota), (tokid_s[:], dtokid), (ident32_s[:], dident),
        ]
        for dst, src in sync_loads:
            pl.op("sync", lambda e, d_=dst, s_=src: e.dma_start(out=d_, in_=s_),
                  inc=(S["ld"], 16))
        LD0 = S["ld"].n
        g_loads = [
            (ident_s[:], dident),
            (gate_s[:].rearrange("p (a f) -> p a f", a=KT),
             dgate.rearrange("(a p) f -> p a f", p=P)),
        ]
        for dst, src in g_loads:
            pl.op("gpsimd", lambda e, d_=dst, s_=src: e.dma_start(out=d_, in_=s_),
                  inc=(S["gld"], 16))
        GLD0 = S["gld"].n

        C = dict(ug=0, ud=0, gsl=0, usl=0, wsl=0, ne=0, tr=0, ptk=0)
        regs = {}

        def _mkregs(e):
            r = e.alloc_register("rb")
            e.reg_mov(r, 8 * CPP)
            regs["b"] = r
        pl.raw("gpsimd", _mkregs)
        state = dict(grel=[], urel=[], rel_down=[], ps_cnt=0, pk_cnt=0,
                     g_group_cnt=0, tile_last_stt={})

        def emit_transposes(src_tile_fn, nt_tiles, prep_fn, xw_entry, rcp_entry):
            base = S["tre"].n
            first = True
            for tt in range(nt_tiles):
                psem, pval = prep_fn(tt)
                pl.wait("tensor", psem, pval)
                if first:
                    pl.wait("tensor", S["xw"], xw_entry)
                    pl.wait("tensor", S["rcp"], rcp_entry)
                    first = False
                for kt in range(KT):
                    k = C["tr"]
                    sl = k % 4
                    if k >= 4:
                        pl.wait("tensor", S["tre"], k - 3)
                    pl.op("tensor", lambda e, s_=src_tile_fn(tt, kt), d_=tslots[sl]:
                          e.transpose(d_, s_, ident_s[:]), inc=(S["trp"], 1))
                    eng = "scalar" if (kt % 2 == 0) else "vector"
                    pl.wait(eng, S["trp"], S["trp"].n)
                    dst = lnT[:, kt * T + tt * P: kt * T + (tt + 1) * P]
                    if eng == "scalar":
                        pl.op(eng, lambda e, d_=dst, s_=tslots[sl]: e.copy(d_, s_),
                              inc=(S["tre"], 1))
                    else:
                        pl.op(eng, lambda e, d_=dst, s_=tslots[sl]:
                              e.tensor_copy(d_, s_), inc=(S["tre"], 1))
                    C["tr"] += 1
            return base

        def ffn_call(ph, xw_entry, rcp_entry, xw_tile=None):
            xw_base = S["xw"].n

            for tt in range(NT):
                pl.wait("vector", S["ld"], 16)
                pl.op("vector", lambda e, t=tt: e.reduce_sum(
                    fsum[:, t:t + 1], xr[:, t * D:(t + 1) * D], axis=AXX),
                    inc=(S["stat"], 1))
            dump = lnT[:, 0:D]
            for tt in range(NT):
                pl.wait("scalar", S["ld"], 16)
                if xw_tile is not None:
                    pl.wait("scalar", S["xw"], xw_tile(tt))
                pl.op("scalar", lambda e, t=tt: e.activation(
                    dump, xr[:, t * D:(t + 1) * D], AF.Square,
                    accum_out=fsq[:, t:t + 1]),
                    inc=(S["stat"], 1))
            pl.wait("vector", S["stat"], S["stat"].n)
            pl.op("vector", lambda e: e.tensor_scalar_mul(fmu[:], fsum[:], 1.0 / D))
            pl.op("vector", lambda e: e.tensor_scalar_mul(fvar[:], fsq[:], 1.0 / D))
            pl.raw("vector", lambda e: e.drain())
            pl.op("vector", lambda e: e.tensor_mul(ftmp[:], fmu[:], fmu[:]))
            pl.raw("vector", lambda e: e.drain())
            pl.op("vector", lambda e: e.tensor_sub(fvar[:], fvar[:], ftmp[:]))
            pl.raw("vector", lambda e: e.drain())
            pl.op("vector", lambda e: e.tensor_scalar_add(ftmp[:], fvar[:], EPS),
                  inc=(S["stat"], 1))
            pl.wait("scalar", S["stat"], S["stat"].n)
            pl.op("scalar", lambda e: e.activation(ftmp[:], ftmp[:], AF.Sqrt),
                  inc=(S["stat"], 1))
            pl.wait("vector", S["stat"], S["stat"].n)
            pl.op("vector", lambda e: e.reciprocal(frstd[:], ftmp[:]))
            pl.raw("vector", lambda e: e.drain())
            pl.op("vector", lambda e: e.tensor_mul(fnmu[:], fmu[:], frstd[:]))
            pl.raw("vector", lambda e: e.drain())
            pl.op("vector", lambda e: e.tensor_scalar_mul(fnmu[:], fnmu[:], -1.0),
                  inc=(S["stat"], 1))
            stats_done = S["stat"].n

            ln1tok = state["ln1tok"]

            lns0 = S["lns"].n
            gbase = state["g_group_cnt"]
            for j in range(IT):
                state["grel"].append(gbase + 2 * (j + 1))
                state["urel"].append(gbase + 2 * (j + 1))
            for pe_ in range(8):
                for j in range(IT):
                    v_ = gbase + 22 + pe_ * IT + j + 1
                    state["grel"].append(v_)
                    state["urel"].append(v_)
            state["g_group_cnt"] = gbase + 22 + 88

            def prep(tt):
                if tt == 0:
                    pl.wait("scalar", S["stat"], stats_done)
                if tt >= 1:
                    pl.wait("scalar", S["trp"], C["tr"])
                    pl.wait("scalar", S["lns"], lns0 + 16 * tt)
                pl.op("scalar", lambda e, t=tt: e.activation(
                    ln1tok[:, 0:D],
                    xr[:, t * D:(t + 1) * D], AF.Identity,
                    bias=fnmu[:, t:t + 1], scale=frstd[:, t:t + 1]),
                    inc=(S["lntok"], 1))
                pl.wait("sync", S["lntok"], S["lntok"].n)
                pl.op("sync", lambda e, t=tt, lt=ln1tok: e.dma_start(
                    out=dln[t * P:(t + 1) * P, :], in_=lt[:, 0:D]),
                    inc=(S["lns"], 16))
                return S["lntok"], S["lntok"].n

            def srcf(tt, kt):
                return ln1tok[:, kt * P:(kt + 1) * P]

            tre_base = emit_transposes(srcf, NT, prep, xw_entry, rcp_entry)

            pl.wait("tensor", S["gld"], GLD0)
            pl.wait("tensor", S["tre"], tre_base + NT * KT)
            for tt in range(NT):
                for kt in range(KT):
                    pl.op("tensor", lambda e, t=tt, k=kt: e.matmul(
                        pD[:, t * 8:(t + 1) * 8],
                        lnT[:, k * T + t * P: k * T + (t + 1) * P],
                        gate_s[:, k * 8:(k + 1) * 8],
                        start=(k == 0), stop=(k == KT - 1)),
                        inc=(S["rt"], 1) if kt == KT - 1 else None)
            pl.wait("vector", S["rt"], S["rt"].n)
            pl.op("vector", lambda e: e.tensor_copy(lg_s[:], pD[:, 0:NT * 8]),
                  inc=(S["rcp"], 1))
            pl.raw("vector", lambda e: e.drain())
            lg3 = lg_s[:].rearrange("p (a f) -> p a f", a=NT)
            pl.op("vector", lambda e: e.reduce_max(m1_s[:], lg3, axis=AXX))
            pl.raw("vector", lambda e: e.drain())
            for tt in range(NT):
                pl.op("vector", lambda e, t=tt: e.tensor_scalar(
                    eq1_s[:, t * 8:(t + 1) * 8], lg_s[:, t * 8:(t + 1) * 8],
                    m1_s[:, t:t + 1], None, ALU.is_equal))
            pl.raw("vector", lambda e: e.drain())
            for tt in range(NT):
                pl.op("vector", lambda e, t=tt: e.scalar_tensor_tensor(
                    msk_s[:, t * 8:(t + 1) * 8], eq1_s[:, t * 8:(t + 1) * 8],
                    -1e30, lg_s[:, t * 8:(t + 1) * 8], ALU.mult, ALU.add))
            pl.raw("vector", lambda e: e.drain())
            msk3 = msk_s[:].rearrange("p (a f) -> p a f", a=NT)
            pl.op("vector", lambda e: e.reduce_max(m2_s[:], msk3, axis=AXX))
            pl.raw("vector", lambda e: e.drain())
            pl.op("vector", lambda e: e.tensor_sub(e2_s[:], m2_s[:], m1_s[:]),
                  inc=(S["stat"], 1))
            pl.wait("scalar", S["stat"], S["stat"].n)
            pl.op("scalar", lambda e: e.activation(e2_s[:], e2_s[:], AF.Exp),
                  inc=(S["stat"], 1))
            pl.wait("vector", S["stat"], S["stat"].n)
            pl.op("vector", lambda e: e.tensor_scalar_add(w1_s[:], e2_s[:], 1.0))
            pl.raw("vector", lambda e: e.drain())
            pl.op("vector", lambda e: e.reciprocal(w1_s[:], w1_s[:]))
            pl.raw("vector", lambda e: e.drain())
            pl.op("vector", lambda e: e.tensor_scalar_mul(w1_s[:], w1_s[:], 0.5))
            pl.raw("vector", lambda e: e.drain())
            pl.op("vector", lambda e: e.tensor_mul(w2_s[:], e2_s[:], w1_s[:]))
            pl.raw("vector", lambda e: e.drain())
            for tt in range(NT):
                pl.op("vector", lambda e, t=tt: e.tensor_scalar(
                    eq2_s[:, t * 8:(t + 1) * 8], msk_s[:, t * 8:(t + 1) * 8],
                    m2_s[:, t:t + 1], None, ALU.is_equal))
            pl.raw("vector", lambda e: e.drain())
            pl.op("vector", lambda e: e.tensor_add(msk_s[:], eq1_s[:], eq2_s[:]),
                  inc=(S["stat"], 1))
            m_done = S["stat"].n

            g_mt = nc.sbuf_tensor("mt%d" % ph, [8, T], F32)
            g_z8 = nc.sbuf_tensor("z8%d" % ph, [8, 512], F32)
            mt_s = g_mt.__enter__()
            zro8 = g_z8.__enter__()
            pl.op("vector", lambda e: e.memset(zro8[:], 0.0))
            mslots = [pD[0:8, 192:320], pC[0:8, 64:192]]
            pslots = [pD[0:P, 320:328], pC[0:P, 192:200]]
            pl.wait("tensor", S["stat"], m_done)
            mtr0 = S["mtr"].n
            for tt in range(NT):
                k = S["mtr"].n - mtr0
                if k >= 2:
                    pl.wait("tensor", S["mte"], S["mte"].n - 1)
                pl.op("tensor", lambda e, t=tt, sl=mslots[tt % 2]: e.transpose(
                    sl, msk_s[:, t * 8:(t + 1) * 8], ident32_s[:]),
                    inc=(S["mtr"], 1))
                pl.wait("vector", S["mtr"], S["mtr"].n)
                pl.op("vector", lambda e, t=tt, sl=mslots[tt % 2]: e.tensor_copy(
                    mt_s[:, t * P:(t + 1) * P], sl), inc=(S["mte"], 1))
            pl.op("vector", lambda e: e.tensor_tensor_scan(
                mt_s[:, 0:512], zro8[:], mt_s[:, 0:512], 0.0, ALU.add, ALU.add))
            pl.raw("vector", lambda e: e.drain())
            pl.op("vector", lambda e: e.tensor_tensor_scan(
                mt_s[:, 512:T], zro8[:], mt_s[:, 512:T], mt_s[:, 511:512],
                ALU.add, ALU.add), inc=(S["mte"], 1))
            scan_done = S["mte"].n
            pl.wait("tensor", S["mte"], scan_done)
            for tt in range(NT):
                if tt >= 2:
                    pl.wait("tensor", S["mte"], S["mte"].n - 1)
                pl.op("tensor", lambda e, t=tt, sl=pslots[tt % 2]: e.transpose(
                    sl, mt_s[:, t * P:(t + 1) * P], ident32_s[0:8, 0:8]),
                    inc=(S["mtr"], 1))
                pl.wait("vector", S["mtr"], S["mtr"].n)
                pl.op("vector", lambda e, t=tt, sl=pslots[tt % 2]: e.tensor_copy(
                    msk_s[:, t * 8:(t + 1) * 8], sl), inc=(S["mte"], 1))
            pl.raw("vector", lambda e: e.drain())

            pl.op("vector", lambda e: e.tensor_mul(lg_s[:], eq1_s[:], msk_s[:]))
            pl.raw("vector", lambda e: e.drain())
            lgr = lg_s[:].rearrange("p (a f) -> p a f", a=NT)
            pl.op("vector", lambda e: e.reduce_sum(m1_s[:], lgr, axis=AXX))
            pl.raw("vector", lambda e: e.drain())
            pl.op("vector", lambda e: e.tensor_mul(lg_s[:], eq2_s[:], msk_s[:]))
            pl.raw("vector", lambda e: e.drain())
            pl.op("vector", lambda e: e.reduce_sum(m2_s[:], lgr, axis=AXX))
            pl.raw("vector", lambda e: e.drain())
            pl.op("vector", lambda e: e.tensor_mul(lg_s[:], eq1_s[:], eiota_s[:]))
            pl.raw("vector", lambda e: e.drain())
            pl.op("vector", lambda e: e.reduce_sum(ftmp[:], lgr, axis=AXX))
            pl.raw("vector", lambda e: e.drain())
            pl.op("vector", lambda e: e.tensor_mul(lg_s[:], eq2_s[:], eiota_s[:]))
            pl.raw("vector", lambda e: e.drain())
            pl.op("vector", lambda e: e.reduce_sum(fvar[:], lgr, axis=AXX))
            pl.raw("vector", lambda e: e.drain())
            for rix, cnt, slt, ovv in ((rix1, ftmp, m1_s, eq1_s),
                                       (rix2, fvar, m2_s, eq2_s)):
                pl.op("vector", lambda e, r=rix, c=cnt: e.tensor_scalar(
                    r[:], c[:], float(CPP), -1.0, ALU.mult, ALU.add))
                pl.raw("vector", lambda e: e.drain())
                pl.op("vector", lambda e, r=rix, s=slt: e.tensor_add(
                    r[:], r[:], s[:]))
                pl.raw("vector", lambda e: e.drain())
                pl.op("vector", lambda e, s=slt: e.tensor_scalar(
                    lg_s[:, 0:NT], s[:], float(CP) + 0.49, None, ALU.is_gt))
                pl.raw("vector", lambda e: e.drain())
                pl.op("vector", lambda e, r=rix: e.scalar_tensor_tensor(
                    r[:], lg_s[:, 0:NT], 1e6, r[:], ALU.mult, ALU.add))
                pl.raw("vector", lambda e: e.drain())
            pl.op("vector", lambda e: e.tensor_copy(ix1[:], rix1[:]))
            pl.op("vector", lambda e: e.tensor_copy(ix2[:], rix2[:]),
                  inc=(S["stat"], 1))
            idx_ready = S["stat"].n

            pl.op("vector", lambda e: e.memset(ids_sb[:], 0.0),
                  inc=(S["stat"], 1))
            ids0 = S["ids"].n
            pl.wait("gpsimd", S["stat"], S["stat"].n)
            pl.op("gpsimd", lambda e: e.dma_start(
                out=dids.rearrange("(p a) f -> p a f", p=P),
                in_=ids_sb[:].rearrange("p (a f) -> p a f", f=1)),
                inc=(S["ids"], 16))
            pl.wait("gpsimd", S["ids"], ids0 + 16)
            for tt in range(NT):
                for ixs in (ix1, ix2):
                    pl.op("gpsimd", lambda e, t=tt, ix=ixs: e.indirect_dma_start(
                        out=dids, out_offset=IndirectOffsetOnAxis(ix[:, t:t + 1], 0),
                        in_=tokid_s[:, t:t + 1], in_offset=None,
                        bounds_check=regs["b"], oob_is_err=False),
                        inc=(S["ids"], 16))
            pl.wait("gpsimd", S["ids"], ids0 + 16 * 17)
            pl.op("gpsimd", lambda e: e.dma_start(
                out=ids_raw[:],
                in_=dids.rearrange("(a p) f -> a (p f)", p=P)),
                inc=(S["ids"], 16))
            mte_at_ids = S["mte"].n

            def emit_ids_pe():
                pl.wait("tensor", S["ids"], ids0 + 16 * 18)
                pl.wait("tensor", S["mte"], mte_at_ids)
                pl.op("tensor", lambda e: e.transpose(
                    pC[0:P, 64:96], ids_raw[:], ident32_s[0:32, 0:32]),
                    inc=(S["mtr"], 1))
                pl.wait("vector", S["mtr"], S["mtr"].n)
                pl.op("vector", lambda e: e.tensor_copy(ids_sb[:],
                                                        pC[0:P, 64:96]))
                pl.raw("vector", lambda e: e.drain())
                pl.op("vector", lambda e: e.tensor_copy(idi_sb[:], ids_sb[:]),
                      inc=(S["ids"], 1))
                state[("idr", ph)] = S["ids"].n

            def emit_slice(e_i, i):
                for (dsrc, buf, cnt, sld, relkey) in (
                        (dwg, wg2, "gsl", "wg", "grel"),
                        (dwu, wu2, "usl", "wu", "urel")):
                    sl = C[cnt]
                    if sl >= 3:
                        pl.wait("gpsimd", S["pg" if relkey == "grel" else "pu"],
                                state[relkey][sl - 3])
                    src = dsrc[e_i][:, i * P:(i + 1) * P].rearrange(
                        "(a p) i -> p a i", p=P)
                    dst = buf[:, (sl % 3) * KT * P:((sl % 3) + 1) * KT * P] \
                        .rearrange("p (a i) -> p a i", a=KT)
                    pl.op("gpsimd", lambda e, d_=dst, s_=src:
                          e.dma_start(out=d_, in_=s_), inc=(S[sld], 16))
                    C[cnt] += 1

            g_z8.__exit__(None, None, None)
            g_mt.__exit__(None, None, None)
            wd = state["wd"]
            g_hsh = nc.sbuf_tensor("hsh%d" % ph, [P, IT * T], BF16)
            h = g_hsh.__enter__()
            lnb = state["ln1tok"]
            head_tr_end = C["tr"]
            lns_all = S["lns"].n
            store_cnt0 = state["ps_cnt"]
            ptr_tile = {}
            pk_tile = {}

            def emit_gather(e_g, q):
                g = 3 * e_g + q
                pl.wait_lazy("gpsimd", S["ids"], state, ("idr", ph))
                pl.wait("gpsimd", S["lns"], lns_all)
                if g >= 2:
                    pl.wait_lazy("gpsimd", S["ptr"], ptr_tile, g - 2)
                elif g == 0:
                    pl.wait("gpsimd", S["trp"], head_tr_end)
                pl.op("gpsimd", lambda e, p_=e_g, qq=q, b=(g % 2):
                      e.indirect_dma_start(
                          out=lnb[:, b * D:(b + 1) * D], out_offset=None,
                          in_=dln,
                          in_offset=IndirectOffsetOnAxis(
                              idi_sb[:, p_ * 4 + qq:p_ * 4 + qq + 1], 0)),
                      inc=(S["pk"], 16))
                pk_tile[g] = S["pk"].n

            def emit_pack_ttile(e_g, q):
                g = 3 * e_g + q
                qoff, rows = QS[q]
                pl.wait_lazy("tensor", S["pk"], pk_tile, g)
                for kt in range(KT):
                    j = C["ptk"]
                    if j >= 1:
                        pl.wait("tensor", S["pte"], j - 1)
                    sl = ptslots[j % 2]
                    pl.op("tensor", lambda e, kk=kt, d_=sl, b=(g % 2):
                          e.transpose(
                              d_,
                              lnb[:, b * D + kk * P: b * D + (kk + 1) * P],
                              ident_s[:]),
                          inc=(S["ptr"], 1))
                    pl.wait("vector", S["ptr"], S["ptr"].n)
                    dst = plnT[:, kt * CP + qoff: kt * CP + qoff + rows]
                    pl.op("vector", lambda e, d_=dst, s_=sl:
                          e.tensor_copy(d_, s_), inc=(S["pte"], 1))
                    C["ptk"] += 1
                ptr_tile[g] = S["ptr"].n

            e_i = NE - 1
            emit_slice(e_i, 0)
            emit_slice(e_i, 1)
            pl.wait("gpsimd", S["pd"], C["ud"])
            if ph == 1:
                pl.wait("gpsimd", S["xw"], xw_entry)
            for i in range(IT):
                src = dwd[e_i][i * P:(i + 1) * P, :]
                dst = wd[:, i * D:(i + 1) * D]
                pl.op("gpsimd", lambda e, d_=dst, s_=src:
                      e.dma_start(out=d_, in_=s_), inc=(S["wd"], 16))
                C["wsl"] += 1
            for i in range(2, IT):
                emit_slice(e_i, i)
            emit_gather(0, 0)
            emit_gather(0, 1)

            for i in range(IT):
                for tc in range(2):
                    ug = C["ug"]
                    gsl_idx = C["gsl"] - IT + i
                    if tc == 0:
                        pl.wait("tensor", S["wg"], 16 * (gsl_idx + 1))
                    if ug >= 2:
                        pl.wait("tensor", S["sg"], ug - 1)
                    if i == 0 and tc == 0:
                        pl.wait("tensor", S["tre"], tre_base + NT * KT)
                    if i == 0 and tc == 1:
                        pl.wait("tensor", S["tre"], tre_base + NT * KT)
                    gslot = pA[:, (ug % 2) * 512:((ug % 2) + 1) * 512]
                    for kt in range(KT):
                        v = pl.op("tensor", lambda e, k=kt, t=tc, sl=gslot,
                                  b=(gsl_idx % 3): e.matmul(
                                      sl,
                                      wg2[:, b * KT * P + k * P:
                                          b * KT * P + (k + 1) * P],
                                      lnT[:, k * T + t * 512:
                                          k * T + (t + 1) * 512],
                                      start=(k == 0), stop=(k == KT - 1)),
                                  inc=(S["pg"], 1) if kt == KT - 1 else None)

                    if tc == 0:
                        pl.wait("tensor", S["wu"], 16 * (gsl_idx + 1))
                    if ug >= 2:
                        pl.wait("tensor", S["h"], ug - 1)
                    uslot = pA[:, 1024 + (ug % 2) * 512:
                               1024 + ((ug % 2) + 1) * 512]
                    for kt in range(KT):
                        v = pl.op("tensor", lambda e, k=kt, t=tc, sl=uslot,
                                  b=(gsl_idx % 3): e.matmul(
                                      sl,
                                      wu2[:, b * KT * P + k * P:
                                          b * KT * P + (k + 1) * P],
                                      lnT[:, k * T + t * 512:
                                          k * T + (t + 1) * 512],
                                      start=(k == 0), stop=(k == KT - 1)),
                                  inc=(S["pu"], 1) if kt == KT - 1 else None)

                    pl.wait("scalar", S["pg"], ug + 1)
                    if ug >= 2:
                        pl.wait("scalar", S["h"], ug - 1)
                    sgslot = sg2[:, (ug % 2) * 512:((ug % 2) + 1) * 512]
                    pl.op("scalar", lambda e, sl=sgslot, g=gslot:
                          e.activation(sl, g, AF.Silu), inc=(S["sg"], 1))
                    pl.wait("vector", S["pu"], ug + 1)
                    pl.wait("vector", S["sg"], ug + 1)
                    hdst = h[:, i * T + tc * 512: i * T + (tc + 1) * 512]
                    pl.op("vector", lambda e, d_=hdst, sl=sgslot, u=uslot:
                          e.tensor_mul(d_, sl, u), inc=(S["h"], 1))
                    C["ug"] += 1
                    if i == 2 and tc == 1:
                        emit_ids_pe()

            pl.wait("tensor", S["h"], C["ug"])
            pl.wait("tensor", S["wd"], 16 * C["wsl"])
            pl.wait("tensor", S["tre"], tre_base + NT * KT)
            sunit = 0
            for t128 in range(NT):
                for hh in range(4):
                    ud = C["ud"]
                    if ud >= 2:
                        sm, vv = state["rel_down"][ud - 2]
                        pl.wait("tensor", S[sm], vv)
                    dslot = pB[:, (ud % 2) * 512:((ud % 2) + 1) * 512]
                    for i in range(IT):
                        pl.op("tensor", lambda e, t=t128, q=hh, ii=i, sl=dslot:
                              e.matmul(
                                  sl,
                                  h[:, ii * T + t * P: ii * T + (t + 1) * P],
                                  wd[:, ii * D + q * 512:
                                     ii * D + (q + 1) * 512],
                                  start=(ii == 0), stop=(ii == IT - 1)),
                              inc=(S["pd"], 1) if i == IT - 1 else None)
                    pl.wait("vector", S["pd"], S["pd"].n)
                    xv = xr[:, t128 * D + hh * 512: t128 * D + (hh + 1) * 512]
                    v = pl.op("vector", lambda e, x=xv, sl=dslot:
                              e.scalar_tensor_tensor(
                                  x, sl, 0.5, x, ALU.mult, ALU.add),
                              inc=(S["xw"], 1))
                    state["rel_down"].append(("xw", v))
                    C["ud"] += 1
                    sunit += 1
            shared_xw_end = S["xw"].n
            g_hsh.__exit__(None, None, None)
            g_plnT = nc.sbuf_tensor("plnT%d" % ph, [P, KT * CP], BF16)
            g_hp = nc.sbuf_tensor("hp%d" % ph, [P, IT * CP], BF16)
            g_pio = nc.sbuf_tensor("pio%d" % ph, [P, 2 * D], BF16)
            plnT = g_plnT.__enter__()
            hp = g_hp.__enter__()
            pio = g_pio.__enter__()

            for pe in range(8):
                if pe == 0:
                    emit_gather(0, 2)
                if pe == 0 or not ilv:
                    for q3 in range(3):
                        emit_pack_ttile(pe, q3)
                emit_slice(pe, 0)
                emit_slice(pe, 1)
                if pe < 7:
                    emit_gather(pe + 1, 0)
                    emit_gather(pe + 1, 1)
                pl.wait("gpsimd", S["pd"], C["ud"])
                for i in range(IT):
                    src = dwd[pe][i * P:(i + 1) * P, :]
                    dst = wd[:, i * D:(i + 1) * D]
                    pl.op("gpsimd", lambda e, d_=dst, s_=src:
                          e.dma_start(out=d_, in_=s_), inc=(S["wd"], 16))
                    C["wsl"] += 1
                for i in range(2, IT):
                    emit_slice(pe, i)
                if pe < 7:
                    emit_gather(pe + 1, 2)

                for i in range(IT):
                    ug = C["ug"]
                    gsl_idx = C["gsl"] - IT + i
                    pl.wait("tensor", S["wg"], 16 * (gsl_idx + 1))
                    if ug >= 2:
                        pl.wait("tensor", S["sg"], ug - 1)
                    if i == 0:
                        pl.wait("tensor", S["pte"], ptr_tile[3 * pe + 2])
                    gslot = pA[:, (ug % 2) * 512:(ug % 2) * 512 + CP]
                    for kt in range(KT):
                        v = pl.op("tensor", lambda e, k=kt, sl=gslot,
                                  b=(gsl_idx % 3): e.matmul(
                                      sl,
                                      wg2[:, b * KT * P + k * P:
                                          b * KT * P + (k + 1) * P],
                                      plnT[:, k * CP:(k + 1) * CP],
                                      start=(k == 0), stop=(k == KT - 1)),
                                  inc=(S["pg"], 1) if kt == KT - 1 else None)

                    pl.wait("tensor", S["wu"], 16 * (gsl_idx + 1))
                    if ug >= 2:
                        pl.wait("tensor", S["h"], ug - 1)
                    uslot = pA[:, 1024 + (ug % 2) * 512:
                               1024 + (ug % 2) * 512 + CP]
                    for kt in range(KT):
                        v = pl.op("tensor", lambda e, k=kt, sl=uslot,
                                  b=(gsl_idx % 3): e.matmul(
                                      sl,
                                      wu2[:, b * KT * P + k * P:
                                          b * KT * P + (k + 1) * P],
                                      plnT[:, k * CP:(k + 1) * CP],
                                      start=(k == 0), stop=(k == KT - 1)),
                                  inc=(S["pu"], 1) if kt == KT - 1 else None)

                    pl.wait("scalar", S["pg"], ug + 1)
                    if ug >= 2:
                        pl.wait("scalar", S["h"], ug - 1)
                    sgslot = sg2[:, (ug % 2) * 512:(ug % 2) * 512 + CP]
                    pl.op("scalar", lambda e, sl=sgslot, g=gslot:
                          e.activation(sl, g, AF.Silu), inc=(S["sg"], 1))
                    pl.wait("vector", S["pu"], ug + 1)
                    pl.wait("vector", S["sg"], ug + 1)
                    hdst = hp[:, i * CP:(i + 1) * CP]
                    pl.op("vector", lambda e, d_=hdst, sl=sgslot, u=uslot:
                          e.tensor_mul(d_, sl, u), inc=(S["h"], 1))
                    C["ug"] += 1

                pl.wait("tensor", S["h"], C["ug"])
                pl.wait("tensor", S["wd"], 16 * C["wsl"])
                unit = 0
                for q, (qoff, rows) in enumerate(QS):
                    gtile = state["ps_cnt"] - store_cnt0
                    for hh in range(4):
                        ud = C["ud"]
                        if ud >= 2:
                            sm, vv = state["rel_down"][ud - 2]
                            pl.wait("tensor", S[sm], vv)
                        dslot = pB[:, (ud % 2) * 512:((ud % 2) + 1) * 512]
                        for i in range(IT):
                            pl.op("tensor", lambda e, qo=qoff, r=rows, q2=hh,
                                  ii=i, sl=dslot: e.matmul(
                                      sl[0:r, :],
                                      hp[:, ii * CP + qo: ii * CP + qo + r],
                                      wd[:, ii * D + q2 * 512:
                                         ii * D + (q2 + 1) * 512],
                                      start=(ii == 0), stop=(ii == IT - 1)),
                                  inc=(S["pd"], 1) if i == IT - 1 else None)
                        pl.wait("vector", S["pd"], S["pd"].n)
                        if state["ps_cnt"] >= store_cnt0 + 2:
                            pl.wait("vector", S["ps"],
                                    16 * (state["ps_cnt"] - 1))
                        pdst = pio[0:rows, (gtile % 2) * D + hh * 512:
                                   (gtile % 2) * D + (hh + 1) * 512]
                        v = pl.op("vector", lambda e, d_=pdst, sl=dslot,
                                  r=rows: e.tensor_copy(d_, sl[0:r, :]),
                                  inc=(S["pv"], 1))
                        state["rel_down"].append(("pv", v))
                        C["ud"] += 1
                        unit += 1
                        if ilv and pe < 7 and unit in (2, 6, 10):
                            emit_pack_ttile(pe + 1, {2: 0, 6: 1, 10: 2}[unit])
                    pl.wait("sync", S["pv"], v)
                    pl.op("sync", lambda e, p_=pe, qo=qoff, r=rows,
                          b=(gtile % 2): e.dma_start(
                              out=dpo[p_ * CPP + qo: p_ * CPP + qo + r, :],
                              in_=pio[0:r, b * D:(b + 1) * D]),
                          inc=(S["ps"], 16))
                    state["ps_cnt"] += 1

            ps_all = 16 * state["ps_cnt"]
            last_stt = {}
            for tt in range(NT):
                for j, (ixs, wcol) in enumerate(((ix1, w1_s), (ix2, w2_s))):
                    buf = pio if tt % 2 == 0 else lnb
                    pl.wait("gpsimd", S["ps"], ps_all)
                    if (tt % 2, j) in last_stt:
                        pl.wait("gpsimd", S["xw"], last_stt[(tt % 2, j)])
                    pslot = buf[:, j * D:(j + 1) * D]
                    vg = pl.op("gpsimd", lambda e, t=tt, ix=ixs, d_=pslot:
                               e.indirect_dma_start(
                                   out=d_, out_offset=None,
                                   in_=dpo,
                                   in_offset=IndirectOffsetOnAxis(
                                       ix[:, t:t + 1], 0),
                                   bounds_check=regs["b"], oob_is_err=False),
                               inc=(S["pk"], 16))
                    pl.wait("vector", S["pk"], vg)
                    xv = xr[:, tt * D:(tt + 1) * D]
                    v = pl.op("vector", lambda e, x=xv, d_=pslot, w=wcol, t=tt:
                              e.scalar_tensor_tensor(
                                  x, d_, w[:, t:t + 1], x, ALU.mult, ALU.add),
                              inc=(S["xw"], 1))
                    last_stt[(tt % 2, j)] = v
                state["tile_last_stt"][(ph, tt)] = v
            g_pio.__exit__(None, None, None)
            g_hp.__exit__(None, None, None)
            g_plnT.__exit__(None, None, None)

        g_tok1 = nc.sbuf_tensor("tok1", [P, 2 * D], BF16)
        g_wd1 = nc.sbuf_tensor("wdA1", [P, IT * D], BF16)
        state["ln1tok"] = g_tok1.__enter__()
        state["wd"] = g_wd1.__enter__()
        ffn_call(0, xw_entry=0, rcp_entry=0)
        F1_XW = S["xw"].n
        g_wd1.__exit__(None, None, None)
        g_tok1.__exit__(None, None, None)
        mamba_xw = F1_XW

        if do_mamba:
            m_guards = []

            def msb(name, shape, dt):
                g = nc.sbuf_tensor(name, list(shape), dt)
                m_guards.append(g)
                return g.__enter__()

            mmu = msb("mmu", [P, NT], F32)
            mvar = msb("mvar", [P, NT], F32)
            mrstd = msb("mrstd", [P, NT], F32)
            mtmp = msb("mtmp", [P, NT], F32)
            mseff = msb("mseff", [P, NT], F32)
            mnb = msb("mnb", [P, NT], F32)
            mnmr = msb("mnmr", [P, NT], F32)
            xpB_s = msb("xpB_s", [DIN, DST], F32)
            xpC_s = msb("xpC_s", [DIN, DST], F32)
            xpd_s = msb("xpd_s", [DIN, DTR], F32)
            dtT_s = msb("dtT_s", [DTR, DIN], F32)
            dtb_s = msb("dtb_s", [DIN, 1], F32)
            convw_s = msb("convw_s", [DIN, 3], F32)
            convb_s = msb("convb_s", [DIN, 1], F32)
            Alog_s = msb("Alog_s", [DIN, DST], F32)
            negA = msb("negA", [DIN, DST], F32)
            Dp_s = msb("Dp_s", [DIN, 1], F32)
            selc_s = msb("selc_s", [DIN, 3], F32)
            selL_s = msb("selL_s", [DIN, 4], F32)
            selR_s = msb("selR_s", [DIN, 4], F32)
            inp_s = msb("inp_s", [P, KT * DIN], BF16)
            outT_s = msb("outT_s", [DIN, D], BF16)
            F_s = msb("F_s", [DIN, DST], F32)
            Atot_s = msb("Atot_s", [DIN, DST], F32)
            carry = msb("carry", [DIN, DST], F32)
            cs1 = msb("cs1", [DIN, DST], F32)
            cs2 = msb("cs2", [DIN, DST], F32)
            cs3 = msb("cs3", [DIN, DST], F32)
            ctmp = msb("ctmp", [DIN, DST], F32)
            slots2 = msb("slots2", [DIN, 4 * 2 * DST], F32)
            slots1 = msb("slots1", [DIN, 4 * 2], F32)
            Stot_s = msb("Stot_s", [DIN, 1], F32)
            zz = msb("zz", [P, D], BF16)
            wbc = msb("wbc", [P, D], BF16)
            bbc = msb("bbc", [P, D], BF16)
            du_s = zz[0:DIN, :].bitcast(F32)
            xscB = wbc[0:DIN, :].bitcast(F32)
            xin = msb("xin", [DIN, T + 2], F32)
            u_s = msb("u_s", [DIN, T], F32)
            delta = msb("delta", [DIN, T], F32)
            dA0 = msb("dA0", [DIN, T], F32)
            dA1 = msb("dA1", [DIN, T], F32)
            dBu = msb("dBu", [DIN, T], F32)
            xsc = msb("xsc", [DIN, T], F32)
            y_s = msb("y_s", [DIN, T], F32)
            Bp_s = msb("Bp_s", [DST, T], F32)
            Cp_s = msb("Cp_s", [DST, T], F32)
            y16 = msb("y16", [DIN, T], BF16)
            bcB = msb("bcB", [DIN, 2 * T], F32)
            bcC = msb("bcC", [DIN, 2 * T], F32)
            dAbufs = [dA0, dA1]

            pl.wait("gpsimd", S["xw"], F1_XW)
            pl.op("gpsimd", lambda e: e.dma_start(out=wbc[:], in_=dlnw),
                  inc=(S["gld"], 16))
            pl.op("gpsimd", lambda e: e.dma_start(out=bbc[:], in_=dlnb),
                  inc=(S["gld"], 16))
            pl.op("gpsimd", lambda e: e.dma_start(
                out=inp_s[:].rearrange("p (a f) -> p a f", a=KT),
                in_=dinp.rearrange("(a p) f -> p a f", p=P)), inc=(S["gld"], 16))
            pl.op("gpsimd", lambda e: e.dma_start(out=outT_s[:], in_=doutT),
                  inc=(S["gld"], 16))
            GLDW = S["gld"].n
            pl.wait("sync", S["xw"], F1_XW)
            for dst, src2 in [(xpB_s[:], dxpB), (xpC_s[:], dxpC),
                              (xpd_s[:], dxpd), (dtT_s[:], ddtT),
                              (dtb_s[:], ddtb), (convw_s[:], dconvw),
                              (convb_s[:], dconvb), (Alog_s[:], dAlog),
                              (Dp_s[:], dDp), (selc_s[:], dselc),
                              (selL_s[:], dselL), (selR_s[:], dselR)]:
                pl.op("sync", lambda e, d_=dst, s_=src2: e.dma_start(
                    out=d_, in_=s_), inc=(S["ld"], 16))
            LDM = S["ld"].n
            pl.wait("scalar", S["ld"], LDM)
            pl.op("scalar", lambda e: e.activation(negA[:], Alog_s[:], AF.Exp),
                  inc=(S["mB"], 1))
            pl.wait("vector", S["mB"], S["mB"].n)
            pl.op("vector", lambda e: e.tensor_scalar_mul(negA[:], negA[:], -1.0),
                  inc=(S["mB"], 1))

            for tt in range(NT):
                pl.op("vector", lambda e, t=tt: e.reduce_sum(
                    fsum[:, t:t + 1], xr[:, t * D:(t + 1) * D], axis=AXX),
                    inc=(S["mA"], 1))
            pl.wait("scalar", S["pu"], S["pu"].n)
            for tt in range(NT):
                pl.wait("scalar", S["xw"], state["tile_last_stt"][(0, tt)])
                pl.op("scalar", lambda e, t=tt: e.activation(
                    lnT[:, 0:D], xr[:, t * D:(t + 1) * D], AF.Square,
                    accum_out=fsq[:, t:t + 1]),
                    inc=(S["mA"], 1))
            pl.wait("vector", S["mA"], S["mA"].n)
            pl.op("vector", lambda e: e.tensor_scalar_mul(mmu[:], fsum[:], 1.0 / D))
            pl.op("vector", lambda e: e.tensor_scalar_mul(mvar[:], fsq[:], 1.0 / D))
            pl.raw("vector", lambda e: e.drain())
            pl.op("vector", lambda e: e.tensor_mul(mtmp[:], mmu[:], mmu[:]))
            pl.raw("vector", lambda e: e.drain())
            pl.op("vector", lambda e: e.tensor_sub(mvar[:], mvar[:], mtmp[:]))
            pl.raw("vector", lambda e: e.drain())
            pl.op("vector", lambda e: e.tensor_scalar_add(mtmp[:], mvar[:], EPS),
                  inc=(S["mB"], 1))
            pl.wait("scalar", S["mB"], S["mB"].n)
            pl.op("scalar", lambda e: e.activation(mtmp[:], mtmp[:], AF.Sqrt),
                  inc=(S["mB"], 1))
            pl.wait("vector", S["mB"], S["mB"].n)
            pl.op("vector", lambda e: e.reciprocal(mrstd[:], mtmp[:]))
            pl.raw("vector", lambda e: e.drain())
            pl.op("vector", lambda e: e.tensor_mul(mtmp[:], mvar[:], mrstd[:]))
            pl.raw("vector", lambda e: e.drain())
            pl.op("vector", lambda e: e.tensor_mul(mtmp[:], mtmp[:], mrstd[:]))
            pl.raw("vector", lambda e: e.drain())
            pl.op("vector", lambda e: e.tensor_scalar_add(mtmp[:], mtmp[:], EPS),
                  inc=(S["mB"], 1))
            pl.wait("scalar", S["mB"], S["mB"].n)
            pl.op("scalar", lambda e: e.activation(mtmp[:], mtmp[:], AF.Sqrt),
                  inc=(S["mB"], 1))
            pl.wait("vector", S["mB"], S["mB"].n)
            pl.op("vector", lambda e: e.reciprocal(mtmp[:], mtmp[:]))
            pl.raw("vector", lambda e: e.drain())
            pl.op("vector", lambda e: e.tensor_mul(mseff[:], mrstd[:], mtmp[:]))
            pl.raw("vector", lambda e: e.drain())
            pl.op("vector", lambda e: e.tensor_mul(mnb[:], mmu[:], mseff[:]))
            pl.op("vector", lambda e: e.tensor_mul(mnmr[:], mmu[:], mrstd[:]))
            pl.raw("vector", lambda e: e.drain())
            pl.op("vector", lambda e: e.tensor_scalar_mul(mnb[:], mnb[:], -1.0))
            pl.op("vector", lambda e: e.tensor_scalar_mul(mnmr[:], mnmr[:], -1.0),
                  inc=(S["mB"], 1))
            mstats = S["mB"].n

            def mprep(tt):
                if tt == 0:
                    pl.wait("scalar", S["mB"], mstats)
                    pl.wait("scalar", S["gld"], GLDW)
                    pl.wait("scalar", S["xw"], F1_XW)
                if tt >= 1:
                    pl.wait("scalar", S["trp"], C["tr"])
                pl.op("scalar", lambda e, t=tt: e.activation(
                    zz[:, 0:D],
                    xr[:, t * D:(t + 1) * D], AF.Identity,
                    bias=mnb[:, t:t + 1], scale=mseff[:, t:t + 1]),
                    inc=(S["mC"], 1))
                pl.wait("vector", S["mC"], S["mC"].n)
                zv = zz[:, 0:D]
                pl.op("vector", lambda e, z=zv: e.tensor_mul(z, z, wbc[:]))
                pl.op("vector", lambda e, z=zv: e.tensor_add(z, z, bbc[:]),
                      inc=(S["mC"], 1))
                return S["mC"], S["mC"].n

            def msrc(tt, kt):
                return zz[:, kt * P:(kt + 1) * P]

            tre_base2 = emit_transposes(msrc, NT, mprep, F1_XW, 1)

            pl.wait("tensor", S["tre"], tre_base2 + NT * KT)
            pl.wait("tensor", S["gld"], GLDW)
            for tc in range(2):
                for kt in range(KT):
                    pl.op("tensor", lambda e, k=kt, t=tc: e.matmul(
                        pA[0:DIN, t * 512:(t + 1) * 512],
                        inp_s[:, k * DIN:(k + 1) * DIN],
                        lnT[:, k * T + t * 512: k * T + (t + 1) * 512],
                        start=(k == 0), stop=(k == KT - 1)),
                        inc=(S["mD"], 1) if (kt == KT - 1) else None)
            pl.wait("scalar", S["mD"], S["mD"].n)
            pl.op("scalar", lambda e: e.copy(xin[:, 1:T + 1], pA[0:DIN, 0:T]),
                  inc=(S["mD"], 1))
            inproj_done = S["mD"].n

            pl.wait("sync", S["mD"], inproj_done)
            pl.op("sync", lambda e: e.dma_start(
                out=dag1i[:, 0:1], in_=xin[:, 1:2],
                allow_slow_non_contiguous=True), inc=(S["ag"], 16))
            pl.op("sync", lambda e: e.dma_start(
                out=dag1i[:, 1:2], in_=xin[:, T:T + 1],
                allow_slow_non_contiguous=True), inc=(S["ag"], 16))
            pl.wait("gpsimd", S["ag"], S["ag"].n)
            pl.op("gpsimd", lambda e: e.collective_compute(
                "AllGather", ALU.bypass, replica_groups=groups,
                ins=[dag1i], outs=[dag1o]), inc=(S["ag"], 1))
            pl.wait("gpsimd", S["ag"], S["ag"].n)
            pl.op("gpsimd", lambda e: e.dma_start(
                out=slots1[:].rearrange("p (r f) -> p r f", r=4),
                in_=dag1o.rearrange("(r p) f -> p r f", r=4)),
                inc=(S["ag"], 16))
            pl.wait("vector", S["ag"], S["ag"].n)
            pl.op("vector", lambda e: e.memset(xin[:, 0:1], 0.0))
            pl.op("vector", lambda e: e.memset(xin[:, T + 1:T + 2], 0.0))
            pl.raw("vector", lambda e: e.drain())
            sl1 = slots1[:].rearrange("p (r f) -> p r f", r=4)
            for r in range(4):
                pl.op("vector", lambda e, rr=r: e.scalar_tensor_tensor(
                    xin[:, 0:1], sl1[:, rr, 1:2], selL_s[:, rr:rr + 1],
                    xin[:, 0:1], ALU.mult, ALU.add))
                pl.raw("vector", lambda e: e.drain())
                pl.op("vector", lambda e, rr=r: e.scalar_tensor_tensor(
                    xin[:, T + 1:T + 2], sl1[:, rr, 0:1], selR_s[:, rr:rr + 1],
                    xin[:, T + 1:T + 2], ALU.mult, ALU.add))
                pl.raw("vector", lambda e: e.drain())

            pl.op("vector", lambda e: e.tensor_scalar(
                delta[:], xin[:, 0:T], convw_s[:, 0:1], None, ALU.mult))
            pl.op("vector", lambda e: e.scalar_tensor_tensor(
                delta[:], xin[:, 1:T + 1], convw_s[:, 1:2], delta[:],
                ALU.mult, ALU.add))
            pl.op("vector", lambda e: e.scalar_tensor_tensor(
                delta[:], xin[:, 2:T + 2], convw_s[:, 2:3], delta[:],
                ALU.mult, ALU.add), inc=(S["mD"], 1))
            pl.wait("scalar", S["mD"], S["mD"].n)
            pl.op("scalar", lambda e: e.activation(
                u_s[:], delta[:], AF.Silu, bias=convb_s[:]), inc=(S["mD"], 1))
            u_done = S["mD"].n

            pl.wait("tensor", S["mD"], u_done)
            for tc in range(2):
                pl.op("tensor", lambda e, t=tc: e.matmul(
                    pA[0:DST, 1024 + t * 512:1024 + (t + 1) * 512],
                    xpB_s[:], u_s[:, t * 512:(t + 1) * 512],
                    start=True, stop=True),
                    inc=(S["mE"], 1) if tc == 1 else None)
                pl.op("tensor", lambda e, t=tc: e.matmul(
                    pB[0:DST, t * 512:(t + 1) * 512],
                    xpC_s[:], u_s[:, t * 512:(t + 1) * 512],
                    start=True, stop=True),
                    inc=(S["mE"], 1) if tc == 1 else None)
            pl.wait("scalar", S["mE"], S["mE"].n)
            pl.op("scalar", lambda e: e.copy(Bp_s[:], pA[0:DST, 1024:2048]),
                  inc=(S["mE"], 1))
            pl.op("scalar", lambda e: e.copy(Cp_s[:], pB[0:DST, 0:T]),
                  inc=(S["mE"], 1))
            bc_evac = S["mE"].n
            pl.wait("tensor", S["mE"], bc_evac)
            for tc in range(2):
                pl.op("tensor", lambda e, t=tc: e.matmul(
                    pA[0:DTR, 1024 + t * 512:1024 + (t + 1) * 512],
                    xpd_s[:], u_s[:, t * 512:(t + 1) * 512],
                    start=True, stop=True),
                    inc=(S["mE"], 1) if tc == 1 else None)
            pl.wait("scalar", S["mE"], S["mE"].n)
            pl.op("scalar", lambda e: e.copy(xsc[0:DTR, :], pA[0:DTR, 1024:2048]),
                  inc=(S["mE"], 1))
            dr_evac = S["mE"].n
            pl.wait("tensor", S["mE"], dr_evac)
            for tc in range(2):
                pl.op("tensor", lambda e, t=tc: e.matmul(
                    pA[0:DIN, t * 512:(t + 1) * 512],
                    dtT_s[:], xsc[0:DTR, t * 512:(t + 1) * 512],
                    start=True, stop=True),
                    inc=(S["mE"], 1) if tc == 1 else None)
            pl.wait("scalar", S["mE"], S["mE"].n)
            pl.op("scalar", lambda e: e.activation(
                dA0[:], pA[0:DIN, 0:T], AF.Abs, bias=dtb_s[:]))
            pl.op("scalar", lambda e: e.activation(
                dA1[:], dA0[:], AF.Exp, scale=-1.0))
            pl.op("scalar", lambda e: e.activation(
                dA1[:], dA1[:], AF.Ln, bias=1.0))
            pl.op("scalar", lambda e: e.activation(
                dA0[:], pA[0:DIN, 0:T], AF.Relu, bias=dtb_s[:]),
                inc=(S["mE"], 1))
            pl.wait("vector", S["mE"], S["mE"].n)
            pl.op("vector", lambda e: e.tensor_add(delta[:], dA0[:], dA1[:]))
            pl.op("vector", lambda e: e.tensor_scalar_mul(delta[:], delta[:],
                                                          0.001))
            pl.raw("vector", lambda e: e.drain())
            pl.op("vector", lambda e: e.tensor_mul(du_s[:], delta[:], u_s[:]))
            pl.raw("vector", lambda e: e.drain())
            pl.op("vector", lambda e: e.reduce_sum(Stot_s[:], delta[:], axis=AXX))
            pl.raw("vector", lambda e: e.drain())
            pl.op("vector", lambda e: e.tensor_scalar(
                Atot_s[:], negA[:], Stot_s[:], None, ALU.mult), inc=(S["mF"], 1))
            pl.wait("scalar", S["mF"], S["mF"].n)
            pl.op("scalar", lambda e: e.activation(Atot_s[:], Atot_s[:], AF.Exp),
                  inc=(S["mF"], 1))
            prep_done = S["mF"].n

            pl.wait("sync", S["mE"], bc_evac)
            pl.op("sync", lambda e: e.dma_start(out=dbpcp[0:DST, :], in_=Bp_s[:]),
                  inc=(S["bcl"], 16))
            pl.op("sync", lambda e: e.dma_start(out=dbpcp[DST:2 * DST, :],
                                                in_=Cp_s[:]),
                  inc=(S["bcl"], 16))

            nbc = 0
            for n in range(DST):
                if nbc >= 2:
                    pl.wait("sync", S["bcf"], nbc - 1)
                bslot = bcB[:, (n % 2) * T:((n % 2) + 1) * T]
                pl.op("sync", lambda e, d_=bslot, r=n: e.dma_start(
                    out=d_, in_=dbpcp[r:r + 1, :].broadcast_to([DIN, T])),
                    inc=(S["bcl"], 16))
                nbc += 1
                if n >= 2:
                    pl.wait("scalar", S["mG"], 3 * (n - 2) + 2)
                pl.op("scalar", lambda e, r=n, b=dAbufs[n % 2]: e.activation(
                    b[:], delta[:], AF.Exp, scale=negA[:, r:r + 1]),
                    inc=(S["mF"], 1))
                pl.wait("vector", S["mF"], prep_done + n + 1)
                pl.wait("vector", S["bcl"], 32 + 16 * nbc)
                pl.op("vector", lambda e, b=bslot: e.tensor_mul(dBu[:], du_s[:],
                                                                b),
                      inc=(S["bcf"], 1), inc2=(S["mG"], 1))
                pl.op("vector", lambda e, b=dAbufs[n % 2]: e.tensor_tensor_scan(
                    xsc[:], b[:], dBu[:], 0.0, ALU.mult, ALU.add),
                    inc=(S["mG"], 1))
                pl.raw("vector", lambda e: e.drain())
                pl.op("vector", lambda e, r=n: e.tensor_copy(
                    F_s[:, r:r + 1], xsc[:, T - 1:T]), inc=(S["mG"], 1))
                pl.raw("vector", lambda e: e.drain())
            pass1_done = S["mG"].n

            pl.wait("sync", S["mG"], pass1_done)
            pl.op("sync", lambda e: e.dma_start(out=dag2i[:, 0:DST], in_=F_s[:]),
                  inc=(S["ag"], 16))
            pl.op("sync", lambda e: e.dma_start(out=dag2i[:, DST:2 * DST],
                                                in_=Atot_s[:]),
                  inc=(S["ag"], 16))
            pl.wait("gpsimd", S["ag"], S["ag"].n)
            pl.op("gpsimd", lambda e: e.collective_compute(
                "AllGather", ALU.bypass, replica_groups=groups,
                ins=[dag2i], outs=[dag2o]), inc=(S["ag"], 1))
            pl.wait("gpsimd", S["ag"], S["ag"].n)
            pl.op("gpsimd", lambda e: e.dma_start(
                out=slots2[:].rearrange("p (r f) -> p r f", r=4),
                in_=dag2o.rearrange("(r p) f -> p r f", r=4)),
                inc=(S["ag"], 16))
            pl.wait("vector", S["ag"], S["ag"].n)
            sl2 = slots2[:].rearrange("p (r f) -> p r f", r=4)
            pl.op("vector", lambda e: e.tensor_copy(cs1[:], sl2[:, 0, 0:DST]))
            pl.raw("vector", lambda e: e.drain())
            pl.op("vector", lambda e: e.tensor_mul(ctmp[:],
                                                   sl2[:, 1, DST:2 * DST], cs1[:]))
            pl.raw("vector", lambda e: e.drain())
            pl.op("vector", lambda e: e.tensor_add(cs2[:], ctmp[:],
                                                   sl2[:, 1, 0:DST]))
            pl.raw("vector", lambda e: e.drain())
            pl.op("vector", lambda e: e.tensor_mul(ctmp[:],
                                                   sl2[:, 2, DST:2 * DST], cs2[:]))
            pl.raw("vector", lambda e: e.drain())
            pl.op("vector", lambda e: e.tensor_add(cs3[:], ctmp[:],
                                                   sl2[:, 2, 0:DST]))
            pl.raw("vector", lambda e: e.drain())
            pl.op("vector", lambda e: e.memset(carry[:], 0.0))
            pl.raw("vector", lambda e: e.drain())
            for j, csj in enumerate((cs1, cs2, cs3)):
                pl.op("vector", lambda e, c=csj, jj=j: e.scalar_tensor_tensor(
                    carry[:], c[:], selc_s[:, jj:jj + 1], carry[:],
                    ALU.mult, ALU.add))
                pl.raw("vector", lambda e: e.drain())
            pl.op("vector", lambda e: e.memset(y_s[:], 0.0), inc=(S["mG"], 1))
            mg_memset = S["mG"].n

            for n in range(DST):
                if nbc >= 2:
                    pl.wait("sync", S["bcf"], nbc - 1)
                bslot = bcB[:, (n % 2) * T:((n % 2) + 1) * T]
                pl.op("sync", lambda e, d_=bslot, r=n: e.dma_start(
                    out=d_, in_=dbpcp[r:r + 1, :].broadcast_to([DIN, T])),
                    inc=(S["bcl"], 16))
                nbc += 1
                if nbc >= 2:
                    pl.wait("sync", S["bcf"], nbc - 1)
                cslot = bcC[:, (n % 2) * T:((n % 2) + 1) * T]
                pl.op("sync", lambda e, d_=cslot, r=n: e.dma_start(
                    out=d_,
                    in_=dbpcp[DST + r:DST + r + 1, :].broadcast_to([DIN, T])),
                    inc=(S["bcl"], 16))
                nbc += 1
                if n >= 2:
                    pl.wait("scalar", S["mH"], 4 * (n - 2) + 2)
                pl.op("scalar", lambda e, r=n, b=dAbufs[n % 2]: e.activation(
                    b[:], delta[:], AF.Exp, scale=negA[:, r:r + 1]),
                    inc=(S["mF"], 1))
                pl.wait("vector", S["mF"], prep_done + DST + n + 1)
                pl.wait("vector", S["bcl"], 32 + 16 * nbc)
                pl.op("vector", lambda e, b=bslot: e.tensor_mul(dBu[:], du_s[:],
                                                                b),
                      inc=(S["bcf"], 1), inc2=(S["mH"], 1))
                pl.op("vector", lambda e, b=dAbufs[n % 2], r=n:
                      e.tensor_tensor_scan(
                          xsc[:], b[:], dBu[:], carry[:, r:r + 1],
                          ALU.mult, ALU.add),
                      inc=(S["mH"], 1))
                pl.op("vector", lambda e, c=cslot: e.tensor_mul(dBu[:], xsc[:], c),
                      inc=(S["bcf"], 1), inc2=(S["mH"], 1))
                pl.op("vector", lambda e: e.tensor_add(y_s[:], y_s[:], dBu[:]),
                      inc=(S["mH"], 1))
            pl.op("vector", lambda e: e.scalar_tensor_tensor(
                y_s[:], u_s[:], Dp_s[:], y_s[:], ALU.mult, ALU.add))
            pl.op("vector", lambda e: e.tensor_copy(y16[:], y_s[:]),
                  inc=(S["mH"], 1))
            y_done = S["mH"].n

            pl.wait("tensor", S["mH"], y_done)
            for tt in range(NT):
                for q in range(4):
                    pl.op("tensor", lambda e, t=tt, qq=q: e.matmul(
                        pA[:, qq * 512:(qq + 1) * 512],
                        y16[:, t * P:(t + 1) * P],
                        outT_s[:, qq * 512:(qq + 1) * 512],
                        start=True, stop=True),
                        inc=(S["mH"], 1) if q == 3 else None)
                pl.wait("vector", S["mH"], S["mH"].n)
                xv = xr[:, tt * D:(tt + 1) * D]
                pl.op("vector", lambda e, x=xv, t=tt: e.scalar_tensor_tensor(
                    x, x, mrstd[:, t:t + 1], x, ALU.mult, ALU.add))
                pl.op("vector", lambda e, x=xv, t=tt: e.tensor_scalar(
                    x, x, mnmr[:, t:t + 1], None, ALU.add))
                vres = pl.op("vector", lambda e, x=xv: e.tensor_add(
                    x, x, pA[:, 0:D]), inc=(S["xw"], 1))
                pl.wait("tensor", S["xw"], vres)
            for g in reversed(m_guards):
                g.__exit__(None, None, None)
            mamba_xw = S["xw"].n

        if do_ffn2:
            g_tok2 = nc.sbuf_tensor("tok2", [P, 2 * D], BF16)
            g_wd2 = nc.sbuf_tensor("wdA2", [P, IT * D], BF16)
            state["ln1tok"] = g_tok2.__enter__()
            state["wd"] = g_wd2.__enter__()
            ffn_call(1, xw_entry=mamba_xw, rcp_entry=1,
                     xw_tile=(lambda tt, base=mamba_xw: base - NT + tt + 1)
                     if do_mamba else None)
            g_wd2.__exit__(None, None, None)
            g_tok2.__exit__(None, None, None)

        if do_ffn2:
            for tt in range(NT):
                pl.wait("sync", S["xw"], state["tile_last_stt"][(1, tt)])
                pl.op("sync", lambda e, t=tt: e.dma_start(
                    out=dy[t * P:(t + 1) * P, :], in_=xr[:, t * D:(t + 1) * D]),
                    inc=(S["st"], 16))
        else:
            pl.wait("sync", S["xw"], S["xw"].n)
            for tt in range(NT):
                pl.op("sync", lambda e, t=tt: e.dma_start(
                    out=dy[t * P:(t + 1) * P, :], in_=xr[:, t * D:(t + 1) * D]),
                    inc=(S["st"], 16))
        pl.wait("sync", S["st"], S["st"].n)

        with nc.Block() as block:
            block.tensor(lambda e: [f(e) for f in pl.q["tensor"]])
            block.vector(lambda e: [f(e) for f in pl.q["vector"]])
            block.scalar(lambda e: [f(e) for f in pl.q["scalar"]])
            block.sync(lambda e: [f(e) for f in pl.q["sync"]])
            block.gpsimd(lambda e: [f(e) for f in pl.q["gpsimd"]])

    return nc


def prep_inputs(inputs):
    f = np.ascontiguousarray
    w = {}
    eg = inputs["L0_ffn_expert_gate_w"]
    eu = inputs["L0_ffn_expert_up_w"]
    ed = inputs["L0_ffn_expert_down_w"]
    sg = inputs["L0_ffn_shared_gate_w"]
    su = inputs["L0_ffn_shared_up_w"]
    sd = inputs["L0_ffn_shared_down_w"]
    w["wgT"] = f(np.concatenate([eg.transpose(0, 2, 1), sg.T[None]], axis=0))
    w["wuT"] = f(np.concatenate([eu.transpose(0, 2, 1), su.T[None]], axis=0))
    w["wdT"] = f(np.concatenate([ed.transpose(0, 2, 1), sd.T[None]], axis=0))
    w["gateT"] = f(inputs["L0_ffn_gate_weight"].T)
    w["inpT"] = f(inputs["L0_attn_in_proj_W"].T)
    xp = inputs["L0_attn_x_proj_W"]
    w["xpd"] = f(xp[0:DTR].T)
    w["xpB"] = f(xp[DTR:DTR + DST].T)
    w["xpC"] = f(xp[DTR + DST:DTR + 2 * DST].T)
    w["dtT"] = f(inputs["L0_attn_dt_proj_W"].T)
    w["dtb"] = f(inputs["L0_attn_dt_proj_b"].reshape(DIN, 1))
    w["convw"] = f(inputs["L0_attn_conv_w"].reshape(DIN, 3))
    w["convb"] = f(inputs["L0_attn_conv_b"].reshape(DIN, 1))
    w["Alog"] = f(inputs["L0_attn_A_log"])
    w["Dp"] = f(inputs["L0_attn_D_param"].reshape(DIN, 1))
    w["outT"] = f(inputs["L0_attn_out_proj_W"].T)
    w["lnw_bc"] = f(np.broadcast_to(np.asarray(inputs["L0_attn_ln_w"])[None, :],
                                    (P, D)))
    w["lnb_bc"] = f(np.broadcast_to(np.asarray(inputs["L0_attn_ln_b"])[None, :],
                                    (P, D)))
    w["ident"] = np.eye(P, dtype=np.float32)
    w["eiota"] = np.ascontiguousarray(
        np.tile(np.arange(8, dtype=np.float32)[None, :], (P, NT)))
    w["tokid"] = np.ascontiguousarray(
        (np.arange(NT, dtype=np.float32)[None, :] * P
         + np.arange(P, dtype=np.float32)[:, None]))
    for k in w:
        w[k] = np.asarray(w[k], dtype=np.float32)
    return w


def make_in_maps(x, w):
    in_maps = []
    for c in range(NCORES):
        b, ch = c // 4, c % 4
        m = dict(w)
        m["x"] = np.ascontiguousarray(x[b, ch * T:(ch + 1) * T, :])
        selc = np.zeros((DIN, 3), np.float32)
        if ch >= 1:
            selc[:, ch - 1] = 1.0
        selL = np.zeros((DIN, 4), np.float32)
        selR = np.zeros((DIN, 4), np.float32)
        if ch > 0:
            selL[:, ch - 1] = 1.0
        if ch < 3:
            selR[:, ch + 1] = 1.0
        m["selc"] = selc
        m["selL"] = selL
        m["selR"] = selR
        in_maps.append(m)
    return in_maps


def kernel(**inputs):
    x = np.asarray(inputs["L0_attn_feat_patch"], dtype=np.float32)
    w = prep_inputs(inputs)
    nc = build_program()
    in_maps = make_in_maps(x, w)
    res = run_bass_kernel_spmd(nc, in_maps, core_ids=list(range(NCORES)))
    out = np.empty((2, 4096, D), dtype=np.float32)
    for c in range(NCORES):
        b, ch = c // 4, c % 4
        out[b, ch * T:(ch + 1) * T, :] = np.asarray(res.results[c]["y"],
                                                    dtype=np.float32)
    return out


if __name__ == "__main__":
    import reference as _r
    inp = _r.setup_inputs()
    got = kernel(**{k: np.asarray(v) for k, v in inp.items()})
    want = np.asarray(_r.reference(**inp))
    err = np.linalg.norm(got - want) / np.linalg.norm(want)
    print("rel_err", err)


# version 14 (final)
# speedup vs baseline: 1.4559x; 1.0949x over previous
import contextlib
import numpy as np
import concourse.bass as bass
import concourse.mybir as mybir
from concourse.bass_utils import run_bass_kernel_spmd
from concourse.bass import IndirectOffsetOnAxis

F32 = mybir.dt.float32
BF16 = mybir.dt.bfloat16
AF = mybir.ActivationFunctionType
ALU = mybir.AluOpType
AXX = mybir.AxisListType.X

P = 128
T = 1024
NT = 8
D = 2048
KT = 16
II = 1408
IT = 11
NE = 9
DIN = 96
DST = 16
DTR = 6
CP = 384
CPP = 512
QS = [(0, 128), (128, 128), (256, 128)]
EPS = 1e-5
NCORES = 8

ENGS = ("tensor", "vector", "scalar", "sync", "gpsimd")


class Sem:
    def __init__(self, es, nc, name):
        self.h = es.enter_context(nc.semaphore(name))
        self.n = 0

    def bump(self, k=1):
        self.n += k
        return self.n


class Plan:
    def __init__(self):
        self.q = {k: [] for k in ENGS}

    def wait(self, eng, sem, val):
        if val <= 0:
            return
        self.q[eng].append(lambda e, s=sem.h, v=val: e.wait_ge(s, v))

    def wait_lazy(self, eng, sem, holder, key):
        self.q[eng].append(lambda e, s=sem.h: e.wait_ge(s, holder[key]))

    def op(self, eng, build, inc=None, inc2=None):
        if inc is None:
            self.q[eng].append(build)
            return None
        sem, k = inc
        v = sem.bump(k)
        self.q[eng].append(lambda e, b=build, s=sem.h, kk=k: b(e).then_inc(s, kk))
        if inc2 is not None:
            sem2, k2 = inc2
            sem2.bump(k2)
            self.q[eng].append(
                lambda e, s2=sem2.h, kk2=k2: e.nop().then_inc(s2, kk2))
        return v

    def raw(self, eng, fn):
        self.q[eng].append(fn)


def build_program(do_mamba=True, do_ffn2=True, n_experts=NE, ilv=True):
    nc = bass.Bass("TRN2", target_bir_lowering=False)

    def din(name, shape):
        return nc.dram_tensor(name, list(shape), F32, kind="ExternalInput").ap()

    dx = din("x", [T, D])
    dwg = din("wgT", [NE, IT, P, KT, P])
    dwu = din("wuT", [NE, IT, P, KT, P])
    dwd = din("wdT", [NE, II, D])
    dgate = din("gateT", [D, 8])
    dinp = din("inpT", [D, DIN])
    dxpB = din("xpB", [DIN, DST])
    dxpC = din("xpC", [DIN, DST])
    dxpd = din("xpd", [DIN, DTR])
    ddtT = din("dtT", [DTR, DIN])
    ddtb = din("dtb", [DIN, 1])
    dconvw = din("convw", [DIN, 3])
    dconvb = din("convb", [DIN, 1])
    dAlog = din("Alog", [DIN, DST])
    dDp = din("Dp", [DIN, 1])
    doutT = din("outT", [DIN, D])
    dlnw = din("lnw_bc", [P, D])
    dlnb = din("lnb_bc", [P, D])
    dident = din("ident", [P, P])
    dselc = din("selc", [DIN, 3])
    dselL = din("selL", [DIN, 4])
    dselR = din("selR", [DIN, 4])
    deiota = din("eiota", [P, NT * 8])
    dtokid = din("tokid", [P, NT])
    dy = nc.dram_tensor("y", [T, D], F32, kind="ExternalOutput").ap()
    dag1i = nc.dram_tensor("ag1i", [DIN, 2], F32, kind="Internal").ap()
    dag1o = nc.dram_tensor("ag1o", [4 * DIN, 2], F32, kind="Internal").ap()
    dag2i = nc.dram_tensor("ag2i", [DIN, 2 * DST], F32, kind="Internal").ap()
    dag2o = nc.dram_tensor("ag2o", [4 * DIN, 2 * DST], F32, kind="Internal").ap()
    dbpcp = nc.dram_tensor("bpcp", [2 * DST, T], F32, kind="Internal").ap()
    dln = nc.dram_tensor("dln", [T, D], BF16, kind="Internal").ap()
    dids = nc.dram_tensor("dids", [8 * CPP, 1], F32, kind="Internal").ap()
    dpo = nc.dram_tensor("dpo", [8 * CPP, D], BF16, kind="Internal").ap()
    groups = [[0, 1, 2, 3], [4, 5, 6, 7]]

    pl = Plan()
    es = contextlib.ExitStack()
    with es:
        sb = lambda name, shape, dt: es.enter_context(
            nc.sbuf_tensor(name, list(shape), dt))
        xr = sb("xr", [P, NT * D], F32)
        lnT = sb("lnT", [P, KT * T], BF16)
        wg2 = sb("wg2", [P, 3 * KT * P], BF16)
        wu2 = sb("wu2", [P, 3 * KT * P], BF16)
        ident_s = sb("ident_s", [P, P], BF16)
        gate_s = sb("gate_s", [P, KT * 8], BF16)
        sg2 = sb("sg2", [P, 2 * 512], F32)
        lg_s = sb("lg_s", [P, NT * 8], F32)
        m1_s = sb("m1_s", [P, NT], F32)
        m2_s = sb("m2_s", [P, NT], F32)
        e2_s = sb("e2_s", [P, NT], F32)
        w1_s = sb("w1_s", [P, NT], F32)
        w2_s = sb("w2_s", [P, NT], F32)
        eq1_s = sb("eq1_s", [P, NT * 8], F32)
        eq2_s = sb("eq2_s", [P, NT * 8], F32)
        msk_s = sb("msk_s", [P, NT * 8], F32)
        eiota_s = sb("eiota_s", [P, NT * 8], F32)
        ident32_s = sb("ident32_s", [P, P], F32)
        tokid_s = sb("tokid_s", [P, NT], F32)
        rix1 = sb("rix1", [P, NT], F32)
        rix2 = sb("rix2", [P, NT], F32)
        ix1 = sb("ix1", [P, NT], mybir.dt.int32)
        ix2 = sb("ix2", [P, NT], mybir.dt.int32)
        ids_sb = sb("ids_sb", [P, 8 * CPP // P], F32)
        ids_raw = sb("ids_raw", [8 * CPP // P, P], F32)
        idi_sb = sb("idi_sb", [P, 8 * CPP // P], mybir.dt.int32)
        fsum = sb("fsum", [P, NT], F32)
        fsq = sb("fsq", [P, NT], F32)
        fmu = sb("fmu", [P, NT], F32)
        fvar = sb("fvar", [P, NT], F32)
        ftmp = sb("ftmp", [P, NT], F32)
        frstd = sb("frstd", [P, NT], F32)
        fnmu = sb("fnmu", [P, NT], F32)

        pA = es.enter_context(nc.psum_tensor("pA", [P, 2048], F32))
        pB = es.enter_context(nc.psum_tensor("pB", [P, 1024], F32))
        pC = es.enter_context(nc.psum_tensor("pC", [P, 512], F32))
        pD = es.enter_context(nc.psum_tensor("pD", [P, 512], F32))
        pB16 = pB[:].bitcast(BF16)
        pC16 = pC[:].bitcast(BF16)
        pD16 = pD[:].bitcast(BF16)
        tslots = [pB16[:, 0:P], pB16[:, 1024:1024 + P],
                  pC16[:, 0:P], pD16[:, 2 * P:3 * P]]
        ptslots = [pC16[:, 0:P], pD16[:, 2 * P:3 * P]]

        S = {}
        for name in ("ld", "gld", "wg", "wgf", "wu", "wuf", "wd", "wdf",
                     "pg", "pu", "sg", "h", "pd", "xw", "trp", "tre",
                     "lntok", "rt", "rcp", "stat",
                     "mA", "mB", "mC", "mD", "mE", "mF", "mG", "mH",
                     "bcl", "bcf", "ag", "st", "mY",
                     "mtr", "mte", "lns", "ids", "pk", "pk2", "pv", "ps",
                     "ptr", "pte"):
            S[name] = Sem(es, nc, "s_" + name)

        sync_loads = [
            (xr[:].rearrange("p (a f) -> p a f", a=NT),
             dx.rearrange("(a p) f -> p a f", p=P)),
        ] + [
            (eiota_s[:], deiota), (tokid_s[:], dtokid), (ident32_s[:], dident),
        ]
        for dst, src in sync_loads:
            pl.op("sync", lambda e, d_=dst, s_=src: e.dma_start(out=d_, in_=s_),
                  inc=(S["ld"], 16))
        LD0 = S["ld"].n
        g_loads = [
            (ident_s[:], dident),
            (gate_s[:].rearrange("p (a f) -> p a f", a=KT),
             dgate.rearrange("(a p) f -> p a f", p=P)),
        ]
        for dst, src in g_loads:
            pl.op("gpsimd", lambda e, d_=dst, s_=src: e.dma_start(out=d_, in_=s_),
                  inc=(S["gld"], 16))
        GLD0 = S["gld"].n

        C = dict(ug=0, ud=0, gsl=0, usl=0, wsl=0, ne=0, tr=0, ptk=0)
        regs = {}

        def _mkregs(e):
            r = e.alloc_register("rb")
            e.reg_mov(r, 8 * CPP)
            regs["b"] = r
        pl.raw("gpsimd", _mkregs)
        state = dict(grel=[], urel=[], rel_down=[], ps_cnt=0, pk_cnt=0,
                     g_group_cnt=0, tile_last_stt={})

        def emit_transposes(src_tile_fn, nt_tiles, prep_fn, xw_entry, rcp_entry):
            base = S["tre"].n
            first = True
            for tt in range(nt_tiles):
                psem, pval = prep_fn(tt)
                pl.wait("tensor", psem, pval)
                if first:
                    pl.wait("tensor", S["xw"], xw_entry)
                    pl.wait("tensor", S["rcp"], rcp_entry)
                    first = False
                for kt in range(KT):
                    k = C["tr"]
                    sl = k % 4
                    if k >= 4:
                        pl.wait("tensor", S["tre"], k - 3)
                    pl.op("tensor", lambda e, s_=src_tile_fn(tt, kt), d_=tslots[sl]:
                          e.transpose(d_, s_, ident_s[:]), inc=(S["trp"], 1))
                    eng = "scalar" if (kt % 2 == 0) else "vector"
                    pl.wait(eng, S["trp"], S["trp"].n)
                    dst = lnT[:, kt * T + tt * P: kt * T + (tt + 1) * P]
                    if eng == "scalar":
                        pl.op(eng, lambda e, d_=dst, s_=tslots[sl]: e.copy(d_, s_),
                              inc=(S["tre"], 1))
                    else:
                        pl.op(eng, lambda e, d_=dst, s_=tslots[sl]:
                              e.tensor_copy(d_, s_), inc=(S["tre"], 1))
                    C["tr"] += 1
            return base

        def ffn_call(ph, xw_entry, rcp_entry, xw_tile=None):
            xw_base = S["xw"].n

            for tt in range(NT):
                pl.wait("vector", S["ld"], 16)
                pl.op("vector", lambda e, t=tt: e.reduce_sum(
                    fsum[:, t:t + 1], xr[:, t * D:(t + 1) * D], axis=AXX),
                    inc=(S["stat"], 1))
            dump = lnT[:, 0:D]
            for tt in range(NT):
                pl.wait("scalar", S["ld"], 16)
                if xw_tile is not None:
                    pl.wait("scalar", S["xw"], xw_tile(tt))
                pl.op("scalar", lambda e, t=tt: e.activation(
                    dump, xr[:, t * D:(t + 1) * D], AF.Square,
                    accum_out=fsq[:, t:t + 1]),
                    inc=(S["stat"], 1))
            pl.wait("vector", S["stat"], S["stat"].n)
            pl.op("vector", lambda e: e.tensor_scalar_mul(fmu[:], fsum[:], 1.0 / D))
            pl.op("vector", lambda e: e.tensor_scalar_mul(fvar[:], fsq[:], 1.0 / D))
            pl.raw("vector", lambda e: e.drain())
            pl.op("vector", lambda e: e.tensor_mul(ftmp[:], fmu[:], fmu[:]))
            pl.raw("vector", lambda e: e.drain())
            pl.op("vector", lambda e: e.tensor_sub(fvar[:], fvar[:], ftmp[:]))
            pl.raw("vector", lambda e: e.drain())
            pl.op("vector", lambda e: e.tensor_scalar_add(ftmp[:], fvar[:], EPS),
                  inc=(S["stat"], 1))
            pl.wait("scalar", S["stat"], S["stat"].n)
            pl.op("scalar", lambda e: e.activation(ftmp[:], ftmp[:], AF.Sqrt),
                  inc=(S["stat"], 1))
            pl.wait("vector", S["stat"], S["stat"].n)
            pl.op("vector", lambda e: e.reciprocal(frstd[:], ftmp[:]))
            pl.raw("vector", lambda e: e.drain())
            pl.op("vector", lambda e: e.tensor_mul(fnmu[:], fmu[:], frstd[:]))
            pl.raw("vector", lambda e: e.drain())
            pl.op("vector", lambda e: e.tensor_scalar_mul(fnmu[:], fnmu[:], -1.0),
                  inc=(S["stat"], 1))
            stats_done = S["stat"].n

            ln1tok = state["ln1tok"]

            lns0 = S["lns"].n
            gbase = state["g_group_cnt"]
            for j in range(IT):
                state["grel"].append(gbase + 2 * (j + 1))
                state["urel"].append(gbase + 2 * (j + 1))
            for pe_ in range(8):
                for j in range(IT):
                    v_ = gbase + 22 + pe_ * IT + j + 1
                    state["grel"].append(v_)
                    state["urel"].append(v_)
            state["g_group_cnt"] = gbase + 22 + 88

            def prep(tt):
                if tt == 0:
                    pl.wait("scalar", S["stat"], stats_done)
                if tt >= 1:
                    pl.wait("scalar", S["trp"], C["tr"])
                    pl.wait("scalar", S["lns"], lns0 + 16 * tt)
                pl.op("scalar", lambda e, t=tt: e.activation(
                    ln1tok[:, 0:D],
                    xr[:, t * D:(t + 1) * D], AF.Identity,
                    bias=fnmu[:, t:t + 1], scale=frstd[:, t:t + 1]),
                    inc=(S["lntok"], 1))
                pl.wait("sync", S["lntok"], S["lntok"].n)
                pl.op("sync", lambda e, t=tt, lt=ln1tok: e.dma_start(
                    out=dln[t * P:(t + 1) * P, :], in_=lt[:, 0:D]),
                    inc=(S["lns"], 16))
                return S["lntok"], S["lntok"].n

            def srcf(tt, kt):
                return ln1tok[:, kt * P:(kt + 1) * P]

            tre_base = emit_transposes(srcf, NT, prep, xw_entry, rcp_entry)

            pl.wait("tensor", S["gld"], GLD0)
            pl.wait("tensor", S["tre"], tre_base + NT * KT)
            for tt in range(NT):
                for kt in range(KT):
                    pl.op("tensor", lambda e, t=tt, k=kt: e.matmul(
                        pD[:, t * 8:(t + 1) * 8],
                        lnT[:, k * T + t * P: k * T + (t + 1) * P],
                        gate_s[:, k * 8:(k + 1) * 8],
                        start=(k == 0), stop=(k == KT - 1)),
                        inc=(S["rt"], 1) if kt == KT - 1 else None)
            pl.wait("vector", S["rt"], S["rt"].n)
            pl.op("vector", lambda e: e.tensor_copy(lg_s[:], pD[:, 0:NT * 8]),
                  inc=(S["rcp"], 1))
            pl.raw("vector", lambda e: e.drain())
            lg3 = lg_s[:].rearrange("p (a f) -> p a f", a=NT)
            pl.op("vector", lambda e: e.reduce_max(m1_s[:], lg3, axis=AXX))
            pl.raw("vector", lambda e: e.drain())
            for tt in range(NT):
                pl.op("vector", lambda e, t=tt: e.tensor_scalar(
                    eq1_s[:, t * 8:(t + 1) * 8], lg_s[:, t * 8:(t + 1) * 8],
                    m1_s[:, t:t + 1], None, ALU.is_equal))
            pl.raw("vector", lambda e: e.drain())
            for tt in range(NT):
                pl.op("vector", lambda e, t=tt: e.scalar_tensor_tensor(
                    msk_s[:, t * 8:(t + 1) * 8], eq1_s[:, t * 8:(t + 1) * 8],
                    -1e30, lg_s[:, t * 8:(t + 1) * 8], ALU.mult, ALU.add))
            pl.raw("vector", lambda e: e.drain())
            msk3 = msk_s[:].rearrange("p (a f) -> p a f", a=NT)
            pl.op("vector", lambda e: e.reduce_max(m2_s[:], msk3, axis=AXX))
            pl.raw("vector", lambda e: e.drain())
            pl.op("vector", lambda e: e.tensor_sub(e2_s[:], m2_s[:], m1_s[:]),
                  inc=(S["stat"], 1))
            pl.wait("scalar", S["stat"], S["stat"].n)
            pl.op("scalar", lambda e: e.activation(e2_s[:], e2_s[:], AF.Exp),
                  inc=(S["stat"], 1))
            pl.wait("vector", S["stat"], S["stat"].n)
            pl.op("vector", lambda e: e.tensor_scalar_add(w1_s[:], e2_s[:], 1.0))
            pl.raw("vector", lambda e: e.drain())
            pl.op("vector", lambda e: e.reciprocal(w1_s[:], w1_s[:]))
            pl.raw("vector", lambda e: e.drain())
            pl.op("vector", lambda e: e.tensor_scalar_mul(w1_s[:], w1_s[:], 0.5))
            pl.raw("vector", lambda e: e.drain())
            pl.op("vector", lambda e: e.tensor_mul(w2_s[:], e2_s[:], w1_s[:]))
            pl.raw("vector", lambda e: e.drain())
            for tt in range(NT):
                pl.op("vector", lambda e, t=tt: e.tensor_scalar(
                    eq2_s[:, t * 8:(t + 1) * 8], msk_s[:, t * 8:(t + 1) * 8],
                    m2_s[:, t:t + 1], None, ALU.is_equal))
            pl.raw("vector", lambda e: e.drain())
            pl.op("vector", lambda e: e.tensor_add(msk_s[:], eq1_s[:], eq2_s[:]),
                  inc=(S["stat"], 1))
            m_done = S["stat"].n

            g_mt = nc.sbuf_tensor("mt%d" % ph, [8, T], F32)
            g_z8 = nc.sbuf_tensor("z8%d" % ph, [8, 512], F32)
            mt_s = g_mt.__enter__()
            zro8 = g_z8.__enter__()
            pl.op("vector", lambda e: e.memset(zro8[:], 0.0))
            mslots = [pD[0:8, 192:320], pC[0:8, 64:192]]
            pslots = [pD[0:P, 320:328], pC[0:P, 192:200]]
            pl.wait("tensor", S["stat"], m_done)
            mtr0 = S["mtr"].n
            for tt in range(NT):
                k = S["mtr"].n - mtr0
                if k >= 2:
                    pl.wait("tensor", S["mte"], S["mte"].n - 1)
                pl.op("tensor", lambda e, t=tt, sl=mslots[tt % 2]: e.transpose(
                    sl, msk_s[:, t * 8:(t + 1) * 8], ident32_s[:]),
                    inc=(S["mtr"], 1))
                pl.wait("vector", S["mtr"], S["mtr"].n)
                pl.op("vector", lambda e, t=tt, sl=mslots[tt % 2]: e.tensor_copy(
                    mt_s[:, t * P:(t + 1) * P], sl), inc=(S["mte"], 1))
            pl.op("vector", lambda e: e.tensor_tensor_scan(
                mt_s[:, 0:512], zro8[:], mt_s[:, 0:512], 0.0, ALU.add, ALU.add))
            pl.raw("vector", lambda e: e.drain())
            pl.op("vector", lambda e: e.tensor_tensor_scan(
                mt_s[:, 512:T], zro8[:], mt_s[:, 512:T], mt_s[:, 511:512],
                ALU.add, ALU.add), inc=(S["mte"], 1))
            scan_done = S["mte"].n
            pl.wait("tensor", S["mte"], scan_done)
            for tt in range(NT):
                if tt >= 2:
                    pl.wait("tensor", S["mte"], S["mte"].n - 1)
                pl.op("tensor", lambda e, t=tt, sl=pslots[tt % 2]: e.transpose(
                    sl, mt_s[:, t * P:(t + 1) * P], ident32_s[0:8, 0:8]),
                    inc=(S["mtr"], 1))
                pl.wait("vector", S["mtr"], S["mtr"].n)
                pl.op("vector", lambda e, t=tt, sl=pslots[tt % 2]: e.tensor_copy(
                    msk_s[:, t * 8:(t + 1) * 8], sl), inc=(S["mte"], 1))
            pl.raw("vector", lambda e: e.drain())

            pl.op("vector", lambda e: e.tensor_mul(lg_s[:], eq1_s[:], msk_s[:]))
            pl.raw("vector", lambda e: e.drain())
            lgr = lg_s[:].rearrange("p (a f) -> p a f", a=NT)
            pl.op("vector", lambda e: e.reduce_sum(m1_s[:], lgr, axis=AXX))
            pl.raw("vector", lambda e: e.drain())
            pl.op("vector", lambda e: e.tensor_mul(lg_s[:], eq2_s[:], msk_s[:]))
            pl.raw("vector", lambda e: e.drain())
            pl.op("vector", lambda e: e.reduce_sum(m2_s[:], lgr, axis=AXX))
            pl.raw("vector", lambda e: e.drain())
            pl.op("vector", lambda e: e.tensor_mul(lg_s[:], eq1_s[:], eiota_s[:]))
            pl.raw("vector", lambda e: e.drain())
            pl.op("vector", lambda e: e.reduce_sum(ftmp[:], lgr, axis=AXX))
            pl.raw("vector", lambda e: e.drain())
            pl.op("vector", lambda e: e.tensor_mul(lg_s[:], eq2_s[:], eiota_s[:]))
            pl.raw("vector", lambda e: e.drain())
            pl.op("vector", lambda e: e.reduce_sum(fvar[:], lgr, axis=AXX))
            pl.raw("vector", lambda e: e.drain())
            for rix, cnt, slt, ovv in ((rix1, ftmp, m1_s, eq1_s),
                                       (rix2, fvar, m2_s, eq2_s)):
                pl.op("vector", lambda e, r=rix, c=cnt: e.tensor_scalar(
                    r[:], c[:], float(CPP), -1.0, ALU.mult, ALU.add))
                pl.raw("vector", lambda e: e.drain())
                pl.op("vector", lambda e, r=rix, s=slt: e.tensor_add(
                    r[:], r[:], s[:]))
                pl.raw("vector", lambda e: e.drain())
                pl.op("vector", lambda e, s=slt: e.tensor_scalar(
                    lg_s[:, 0:NT], s[:], float(CP) + 0.49, None, ALU.is_gt))
                pl.raw("vector", lambda e: e.drain())
                pl.op("vector", lambda e, r=rix: e.scalar_tensor_tensor(
                    r[:], lg_s[:, 0:NT], 1e6, r[:], ALU.mult, ALU.add))
                pl.raw("vector", lambda e: e.drain())
            pl.op("vector", lambda e: e.tensor_copy(ix1[:], rix1[:]))
            pl.op("vector", lambda e: e.tensor_copy(ix2[:], rix2[:]),
                  inc=(S["stat"], 1))
            idx_ready = S["stat"].n

            pl.op("vector", lambda e: e.memset(ids_sb[:], 0.0),
                  inc=(S["stat"], 1))
            ids0 = S["ids"].n
            pl.wait("gpsimd", S["stat"], S["stat"].n)
            pl.op("gpsimd", lambda e: e.dma_start(
                out=dids.rearrange("(p a) f -> p a f", p=P),
                in_=ids_sb[:].rearrange("p (a f) -> p a f", f=1)),
                inc=(S["ids"], 16))
            pl.wait("gpsimd", S["ids"], ids0 + 16)
            for tt in range(NT):
                for ixs in (ix1, ix2):
                    pl.op("gpsimd", lambda e, t=tt, ix=ixs: e.indirect_dma_start(
                        out=dids, out_offset=IndirectOffsetOnAxis(ix[:, t:t + 1], 0),
                        in_=tokid_s[:, t:t + 1], in_offset=None,
                        bounds_check=regs["b"], oob_is_err=False),
                        inc=(S["ids"], 16))
            pl.wait("gpsimd", S["ids"], ids0 + 16 * 17)
            pl.op("gpsimd", lambda e: e.dma_start(
                out=ids_raw[:],
                in_=dids.rearrange("(a p) f -> a (p f)", p=P)),
                inc=(S["ids"], 16))
            mte_at_ids = S["mte"].n

            def emit_ids_pe():
                pl.wait("tensor", S["ids"], ids0 + 16 * 18)
                pl.wait("tensor", S["mte"], mte_at_ids)
                pl.op("tensor", lambda e: e.transpose(
                    pC[0:P, 64:96], ids_raw[:], ident32_s[0:32, 0:32]),
                    inc=(S["mtr"], 1))
                pl.wait("vector", S["mtr"], S["mtr"].n)
                pl.op("vector", lambda e: e.tensor_copy(ids_sb[:],
                                                        pC[0:P, 64:96]))
                pl.raw("vector", lambda e: e.drain())
                pl.op("vector", lambda e: e.tensor_copy(idi_sb[:], ids_sb[:]),
                      inc=(S["ids"], 1))
                state[("idr", ph)] = S["ids"].n

            def emit_slice(e_i, i):
                for (dsrc, buf, cnt, sld, relkey) in (
                        (dwg, wg2, "gsl", "wg", "grel"),
                        (dwu, wu2, "usl", "wu", "urel")):
                    sl = C[cnt]
                    if sl >= 3:
                        pl.wait("gpsimd", S["pg" if relkey == "grel" else "pu"],
                                state[relkey][sl - 3])
                    src = dsrc[e_i][i]
                    dst = buf[:, (sl % 3) * KT * P:((sl % 3) + 1) * KT * P] \
                        .rearrange("p (a i) -> p a i", a=KT)
                    pl.op("gpsimd", lambda e, d_=dst, s_=src:
                          e.dma_start(out=d_, in_=s_), inc=(S[sld], 16))
                    C[cnt] += 1

            g_z8.__exit__(None, None, None)
            g_mt.__exit__(None, None, None)
            wd = state["wd"]
            g_hsh = nc.sbuf_tensor("hsh%d" % ph, [P, IT * T], BF16)
            h = g_hsh.__enter__()
            lnb = state["ln1tok"]
            head_tr_end = C["tr"]
            lns_all = S["lns"].n
            store_cnt0 = state["ps_cnt"]
            ptr_tile = {}
            pk_tile = {}

            def emit_gather(e_g, q):
                g = 3 * e_g + q
                pl.wait_lazy("gpsimd", S["ids"], state, ("idr", ph))
                pl.wait("gpsimd", S["lns"], lns_all)
                if g >= 2:
                    pl.wait_lazy("gpsimd", S["ptr"], ptr_tile, g - 2)
                elif g == 0:
                    pl.wait("gpsimd", S["trp"], head_tr_end)
                pl.op("gpsimd", lambda e, p_=e_g, qq=q, b=(g % 2):
                      e.indirect_dma_start(
                          out=lnb[:, b * D:(b + 1) * D], out_offset=None,
                          in_=dln,
                          in_offset=IndirectOffsetOnAxis(
                              idi_sb[:, p_ * 4 + qq:p_ * 4 + qq + 1], 0)),
                      inc=(S["pk"], 16))
                pk_tile[g] = S["pk"].n

            def emit_pack_ttile(e_g, q):
                g = 3 * e_g + q
                qoff, rows = QS[q]
                pl.wait_lazy("tensor", S["pk"], pk_tile, g)
                for kt in range(KT):
                    j = C["ptk"]
                    if j >= 1:
                        pl.wait("tensor", S["pte"], j - 1)
                    sl = ptslots[j % 2]
                    pl.op("tensor", lambda e, kk=kt, d_=sl, b=(g % 2):
                          e.transpose(
                              d_,
                              lnb[:, b * D + kk * P: b * D + (kk + 1) * P],
                              ident_s[:]),
                          inc=(S["ptr"], 1))
                    pl.wait("vector", S["ptr"], S["ptr"].n)
                    dst = plnT[:, kt * CP + qoff: kt * CP + qoff + rows]
                    pl.op("vector", lambda e, d_=dst, s_=sl:
                          e.tensor_copy(d_, s_), inc=(S["pte"], 1))
                    C["ptk"] += 1
                ptr_tile[g] = S["ptr"].n

            e_i = NE - 1
            emit_slice(e_i, 0)
            emit_slice(e_i, 1)
            pl.wait("gpsimd", S["pd"], C["ud"])
            if ph == 1:
                pl.wait("gpsimd", S["xw"], xw_entry)
            for i in range(IT):
                src = dwd[e_i][i * P:(i + 1) * P, :]
                dst = wd[:, i * D:(i + 1) * D]
                pl.op("gpsimd", lambda e, d_=dst, s_=src:
                      e.dma_start(out=d_, in_=s_), inc=(S["wd"], 16))
                C["wsl"] += 1
            for i in range(2, IT):
                emit_slice(e_i, i)
            emit_gather(0, 0)
            emit_gather(0, 1)

            for i in range(IT):
                for tc in range(2):
                    ug = C["ug"]
                    gsl_idx = C["gsl"] - IT + i
                    if tc == 0:
                        pl.wait("tensor", S["wg"], 16 * (gsl_idx + 1))
                    if ug >= 2:
                        pl.wait("tensor", S["sg"], ug - 1)
                    if i == 0 and tc == 0:
                        pl.wait("tensor", S["tre"], tre_base + NT * KT)
                    if i == 0 and tc == 1:
                        pl.wait("tensor", S["tre"], tre_base + NT * KT)
                    gslot = pA[:, (ug % 2) * 512:((ug % 2) + 1) * 512]
                    for kt in range(KT):
                        v = pl.op("tensor", lambda e, k=kt, t=tc, sl=gslot,
                                  b=(gsl_idx % 3): e.matmul(
                                      sl,
                                      wg2[:, b * KT * P + k * P:
                                          b * KT * P + (k + 1) * P],
                                      lnT[:, k * T + t * 512:
                                          k * T + (t + 1) * 512],
                                      start=(k == 0), stop=(k == KT - 1)),
                                  inc=(S["pg"], 1) if kt == KT - 1 else None)

                    if tc == 0:
                        pl.wait("tensor", S["wu"], 16 * (gsl_idx + 1))
                    if ug >= 2:
                        pl.wait("tensor", S["h"], ug - 1)
                    uslot = pA[:, 1024 + (ug % 2) * 512:
                               1024 + ((ug % 2) + 1) * 512]
                    for kt in range(KT):
                        v = pl.op("tensor", lambda e, k=kt, t=tc, sl=uslot,
                                  b=(gsl_idx % 3): e.matmul(
                                      sl,
                                      wu2[:, b * KT * P + k * P:
                                          b * KT * P + (k + 1) * P],
                                      lnT[:, k * T + t * 512:
                                          k * T + (t + 1) * 512],
                                      start=(k == 0), stop=(k == KT - 1)),
                                  inc=(S["pu"], 1) if kt == KT - 1 else None)

                    pl.wait("scalar", S["pg"], ug + 1)
                    if ug >= 2:
                        pl.wait("scalar", S["h"], ug - 1)
                    sgslot = sg2[:, (ug % 2) * 512:((ug % 2) + 1) * 512]
                    pl.op("scalar", lambda e, sl=sgslot, g=gslot:
                          e.activation(sl, g, AF.Silu), inc=(S["sg"], 1))
                    pl.wait("vector", S["pu"], ug + 1)
                    pl.wait("vector", S["sg"], ug + 1)
                    hdst = h[:, i * T + tc * 512: i * T + (tc + 1) * 512]
                    pl.op("vector", lambda e, d_=hdst, sl=sgslot, u=uslot:
                          e.tensor_mul(d_, sl, u), inc=(S["h"], 1))
                    C["ug"] += 1
                    if i == 2 and tc == 1:
                        emit_ids_pe()

            pl.wait("tensor", S["h"], C["ug"])
            pl.wait("tensor", S["wd"], 16 * C["wsl"])
            pl.wait("tensor", S["tre"], tre_base + NT * KT)
            sunit = 0
            for t128 in range(NT):
                for hh in range(4):
                    ud = C["ud"]
                    if ud >= 2:
                        sm, vv = state["rel_down"][ud - 2]
                        pl.wait("tensor", S[sm], vv)
                    dslot = pB[:, (ud % 2) * 512:((ud % 2) + 1) * 512]
                    for i in range(IT):
                        pl.op("tensor", lambda e, t=t128, q=hh, ii=i, sl=dslot:
                              e.matmul(
                                  sl,
                                  h[:, ii * T + t * P: ii * T + (t + 1) * P],
                                  wd[:, ii * D + q * 512:
                                     ii * D + (q + 1) * 512],
                                  start=(ii == 0), stop=(ii == IT - 1)),
                              inc=(S["pd"], 1) if i == IT - 1 else None)
                    pl.wait("vector", S["pd"], S["pd"].n)
                    xv = xr[:, t128 * D + hh * 512: t128 * D + (hh + 1) * 512]
                    v = pl.op("vector", lambda e, x=xv, sl=dslot:
                              e.scalar_tensor_tensor(
                                  x, sl, 0.5, x, ALU.mult, ALU.add),
                              inc=(S["xw"], 1))
                    state["rel_down"].append(("xw", v))
                    C["ud"] += 1
                    sunit += 1
            shared_xw_end = S["xw"].n
            g_hsh.__exit__(None, None, None)
            g_plnT = nc.sbuf_tensor("plnT%d" % ph, [P, KT * CP], BF16)
            g_hp = nc.sbuf_tensor("hp%d" % ph, [P, IT * CP], BF16)
            g_pio = nc.sbuf_tensor("pio%d" % ph, [P, 2 * D], BF16)
            plnT = g_plnT.__enter__()
            hp = g_hp.__enter__()
            pio = g_pio.__enter__()

            for pe in range(8):
                if pe == 0:
                    emit_gather(0, 2)
                if pe == 0 or not ilv:
                    for q3 in range(3):
                        emit_pack_ttile(pe, q3)
                emit_slice(pe, 0)
                emit_slice(pe, 1)
                if pe < 7:
                    emit_gather(pe + 1, 0)
                    emit_gather(pe + 1, 1)
                pl.wait("gpsimd", S["pd"], C["ud"])
                for i in range(IT):
                    src = dwd[pe][i * P:(i + 1) * P, :]
                    dst = wd[:, i * D:(i + 1) * D]
                    pl.op("gpsimd", lambda e, d_=dst, s_=src:
                          e.dma_start(out=d_, in_=s_), inc=(S["wd"], 16))
                    C["wsl"] += 1
                for i in range(2, IT):
                    emit_slice(pe, i)
                if pe < 7:
                    emit_gather(pe + 1, 2)

                for i in range(IT):
                    ug = C["ug"]
                    gsl_idx = C["gsl"] - IT + i
                    pl.wait("tensor", S["wg"], 16 * (gsl_idx + 1))
                    if ug >= 2:
                        pl.wait("tensor", S["sg"], ug - 1)
                    if i == 0:
                        pl.wait("tensor", S["pte"], ptr_tile[3 * pe + 2])
                    gslot = pA[:, (ug % 2) * 512:(ug % 2) * 512 + CP]
                    for kt in range(KT):
                        v = pl.op("tensor", lambda e, k=kt, sl=gslot,
                                  b=(gsl_idx % 3): e.matmul(
                                      sl,
                                      wg2[:, b * KT * P + k * P:
                                          b * KT * P + (k + 1) * P],
                                      plnT[:, k * CP:(k + 1) * CP],
                                      start=(k == 0), stop=(k == KT - 1)),
                                  inc=(S["pg"], 1) if kt == KT - 1 else None)

                    pl.wait("tensor", S["wu"], 16 * (gsl_idx + 1))
                    if ug >= 2:
                        pl.wait("tensor", S["h"], ug - 1)
                    uslot = pA[:, 1024 + (ug % 2) * 512:
                               1024 + (ug % 2) * 512 + CP]
                    for kt in range(KT):
                        v = pl.op("tensor", lambda e, k=kt, sl=uslot,
                                  b=(gsl_idx % 3): e.matmul(
                                      sl,
                                      wu2[:, b * KT * P + k * P:
                                          b * KT * P + (k + 1) * P],
                                      plnT[:, k * CP:(k + 1) * CP],
                                      start=(k == 0), stop=(k == KT - 1)),
                                  inc=(S["pu"], 1) if kt == KT - 1 else None)

                    pl.wait("scalar", S["pg"], ug + 1)
                    if ug >= 2:
                        pl.wait("scalar", S["h"], ug - 1)
                    sgslot = sg2[:, (ug % 2) * 512:(ug % 2) * 512 + CP]
                    pl.op("scalar", lambda e, sl=sgslot, g=gslot:
                          e.activation(sl, g, AF.Silu), inc=(S["sg"], 1))
                    pl.wait("vector", S["pu"], ug + 1)
                    pl.wait("vector", S["sg"], ug + 1)
                    hdst = hp[:, i * CP:(i + 1) * CP]
                    pl.op("vector", lambda e, d_=hdst, sl=sgslot, u=uslot:
                          e.tensor_mul(d_, sl, u), inc=(S["h"], 1))
                    C["ug"] += 1

                pl.wait("tensor", S["h"], C["ug"])
                pl.wait("tensor", S["wd"], 16 * C["wsl"])
                unit = 0
                for q, (qoff, rows) in enumerate(QS):
                    gtile = state["ps_cnt"] - store_cnt0
                    for hh in range(4):
                        ud = C["ud"]
                        if ud >= 2:
                            sm, vv = state["rel_down"][ud - 2]
                            pl.wait("tensor", S[sm], vv)
                        dslot = pB[:, (ud % 2) * 512:((ud % 2) + 1) * 512]
                        for i in range(IT):
                            pl.op("tensor", lambda e, qo=qoff, r=rows, q2=hh,
                                  ii=i, sl=dslot: e.matmul(
                                      sl[0:r, :],
                                      hp[:, ii * CP + qo: ii * CP + qo + r],
                                      wd[:, ii * D + q2 * 512:
                                         ii * D + (q2 + 1) * 512],
                                      start=(ii == 0), stop=(ii == IT - 1)),
                                  inc=(S["pd"], 1) if i == IT - 1 else None)
                        pl.wait("vector", S["pd"], S["pd"].n)
                        if state["ps_cnt"] >= store_cnt0 + 2:
                            pl.wait("vector", S["ps"],
                                    16 * (state["ps_cnt"] - 1))
                        pdst = pio[0:rows, (gtile % 2) * D + hh * 512:
                                   (gtile % 2) * D + (hh + 1) * 512]
                        v = pl.op("vector", lambda e, d_=pdst, sl=dslot,
                                  r=rows: e.tensor_copy(d_, sl[0:r, :]),
                                  inc=(S["pv"], 1))
                        state["rel_down"].append(("pv", v))
                        C["ud"] += 1
                        unit += 1
                        if ilv and pe < 7 and unit in (2, 6, 10):
                            emit_pack_ttile(pe + 1, {2: 0, 6: 1, 10: 2}[unit])
                    pl.wait("sync", S["pv"], v)
                    pl.op("sync", lambda e, p_=pe, qo=qoff, r=rows,
                          b=(gtile % 2): e.dma_start(
                              out=dpo[p_ * CPP + qo: p_ * CPP + qo + r, :],
                              in_=pio[0:r, b * D:(b + 1) * D]),
                          inc=(S["ps"], 16))
                    state["ps_cnt"] += 1

            ps_all = 16 * state["ps_cnt"]
            last_stt = {}
            for tt in range(NT):
                for j, (ixs, wcol) in enumerate(((ix1, w1_s), (ix2, w2_s))):
                    buf = pio if tt % 2 == 0 else lnb
                    pl.wait("gpsimd", S["ps"], ps_all)
                    if (tt % 2, j) in last_stt:
                        pl.wait("gpsimd", S["xw"], last_stt[(tt % 2, j)])
                    pslot = buf[:, j * D:(j + 1) * D]
                    vg = pl.op("gpsimd", lambda e, t=tt, ix=ixs, d_=pslot:
                               e.indirect_dma_start(
                                   out=d_, out_offset=None,
                                   in_=dpo,
                                   in_offset=IndirectOffsetOnAxis(
                                       ix[:, t:t + 1], 0),
                                   bounds_check=regs["b"], oob_is_err=False),
                               inc=(S["pk"], 16))
                    pl.wait("vector", S["pk"], vg)
                    xv = xr[:, tt * D:(tt + 1) * D]
                    v = pl.op("vector", lambda e, x=xv, d_=pslot, w=wcol, t=tt:
                              e.scalar_tensor_tensor(
                                  x, d_, w[:, t:t + 1], x, ALU.mult, ALU.add),
                              inc=(S["xw"], 1))
                    last_stt[(tt % 2, j)] = v
                state["tile_last_stt"][(ph, tt)] = v
            g_pio.__exit__(None, None, None)
            g_hp.__exit__(None, None, None)
            g_plnT.__exit__(None, None, None)

        g_tok1 = nc.sbuf_tensor("tok1", [P, 2 * D], BF16)
        g_wd1 = nc.sbuf_tensor("wdA1", [P, IT * D], BF16)
        state["ln1tok"] = g_tok1.__enter__()
        state["wd"] = g_wd1.__enter__()
        ffn_call(0, xw_entry=0, rcp_entry=0)
        F1_XW = S["xw"].n
        g_wd1.__exit__(None, None, None)
        g_tok1.__exit__(None, None, None)
        mamba_xw = F1_XW

        if do_mamba:
            m_guards = []

            def msb(name, shape, dt):
                g = nc.sbuf_tensor(name, list(shape), dt)
                m_guards.append(g)
                return g.__enter__()

            mmu = msb("mmu", [P, NT], F32)
            mvar = msb("mvar", [P, NT], F32)
            mrstd = msb("mrstd", [P, NT], F32)
            mtmp = msb("mtmp", [P, NT], F32)
            mseff = msb("mseff", [P, NT], F32)
            mnb = msb("mnb", [P, NT], F32)
            mnmr = msb("mnmr", [P, NT], F32)
            xpB_s = msb("xpB_s", [DIN, DST], F32)
            xpC_s = msb("xpC_s", [DIN, DST], F32)
            xpd_s = msb("xpd_s", [DIN, DTR], F32)
            dtT_s = msb("dtT_s", [DTR, DIN], F32)
            dtb_s = msb("dtb_s", [DIN, 1], F32)
            convw_s = msb("convw_s", [DIN, 3], F32)
            convb_s = msb("convb_s", [DIN, 1], F32)
            Alog_s = msb("Alog_s", [DIN, DST], F32)
            negA = msb("negA", [DIN, DST], F32)
            Dp_s = msb("Dp_s", [DIN, 1], F32)
            selc_s = msb("selc_s", [DIN, 3], F32)
            selL_s = msb("selL_s", [DIN, 4], F32)
            selR_s = msb("selR_s", [DIN, 4], F32)
            inp_s = msb("inp_s", [P, KT * DIN], BF16)
            outT_s = msb("outT_s", [DIN, D], BF16)
            F_s = msb("F_s", [DIN, DST], F32)
            Atot_s = msb("Atot_s", [DIN, DST], F32)
            carry = msb("carry", [DIN, DST], F32)
            cs1 = msb("cs1", [DIN, DST], F32)
            cs2 = msb("cs2", [DIN, DST], F32)
            cs3 = msb("cs3", [DIN, DST], F32)
            ctmp = msb("ctmp", [DIN, DST], F32)
            slots2 = msb("slots2", [DIN, 4 * 2 * DST], F32)
            slots1 = msb("slots1", [DIN, 4 * 2], F32)
            Stot_s = msb("Stot_s", [DIN, 1], F32)
            zz = msb("zz", [P, D], BF16)
            wbc = msb("wbc", [P, D], BF16)
            bbc = msb("bbc", [P, D], BF16)
            du_s = zz[0:DIN, :].bitcast(F32)
            xscB = wbc[0:DIN, :].bitcast(F32)
            xin = msb("xin", [DIN, T + 2], F32)
            u_s = msb("u_s", [DIN, T], F32)
            delta = msb("delta", [DIN, T], F32)
            dA0 = msb("dA0", [DIN, T], F32)
            dA1 = msb("dA1", [DIN, T], F32)
            dBu = msb("dBu", [DIN, T], F32)
            xsc = msb("xsc", [DIN, T], F32)
            y_s = msb("y_s", [DIN, T], F32)
            Bp_s = msb("Bp_s", [DST, T], F32)
            Cp_s = msb("Cp_s", [DST, T], F32)
            y16 = msb("y16", [DIN, T], BF16)
            bcB = msb("bcB", [DIN, 2 * T], F32)
            bcC = msb("bcC", [DIN, 2 * T], F32)
            dAbufs = [dA0, dA1]

            pl.wait("gpsimd", S["xw"], F1_XW)
            pl.op("gpsimd", lambda e: e.dma_start(out=wbc[:], in_=dlnw),
                  inc=(S["gld"], 16))
            pl.op("gpsimd", lambda e: e.dma_start(out=bbc[:], in_=dlnb),
                  inc=(S["gld"], 16))
            pl.op("gpsimd", lambda e: e.dma_start(
                out=inp_s[:].rearrange("p (a f) -> p a f", a=KT),
                in_=dinp.rearrange("(a p) f -> p a f", p=P)), inc=(S["gld"], 16))
            pl.op("gpsimd", lambda e: e.dma_start(out=outT_s[:], in_=doutT),
                  inc=(S["gld"], 16))
            GLDW = S["gld"].n
            pl.wait("sync", S["xw"], F1_XW)
            for dst, src2 in [(xpB_s[:], dxpB), (xpC_s[:], dxpC),
                              (xpd_s[:], dxpd), (dtT_s[:], ddtT),
                              (dtb_s[:], ddtb), (convw_s[:], dconvw),
                              (convb_s[:], dconvb), (Alog_s[:], dAlog),
                              (Dp_s[:], dDp), (selc_s[:], dselc),
                              (selL_s[:], dselL), (selR_s[:], dselR)]:
                pl.op("sync", lambda e, d_=dst, s_=src2: e.dma_start(
                    out=d_, in_=s_), inc=(S["ld"], 16))
            LDM = S["ld"].n
            pl.wait("scalar", S["ld"], LDM)
            pl.op("scalar", lambda e: e.activation(negA[:], Alog_s[:], AF.Exp),
                  inc=(S["mB"], 1))
            pl.wait("vector", S["mB"], S["mB"].n)
            pl.op("vector", lambda e: e.tensor_scalar_mul(negA[:], negA[:], -1.0),
                  inc=(S["mB"], 1))

            for tt in range(NT):
                pl.op("vector", lambda e, t=tt: e.reduce_sum(
                    fsum[:, t:t + 1], xr[:, t * D:(t + 1) * D], axis=AXX),
                    inc=(S["mA"], 1))
            pl.wait("scalar", S["pu"], S["pu"].n)
            for tt in range(NT):
                pl.wait("scalar", S["xw"], state["tile_last_stt"][(0, tt)])
                pl.op("scalar", lambda e, t=tt: e.activation(
                    lnT[:, 0:D], xr[:, t * D:(t + 1) * D], AF.Square,
                    accum_out=fsq[:, t:t + 1]),
                    inc=(S["mA"], 1))
            pl.wait("vector", S["mA"], S["mA"].n)
            pl.op("vector", lambda e: e.tensor_scalar_mul(mmu[:], fsum[:], 1.0 / D))
            pl.op("vector", lambda e: e.tensor_scalar_mul(mvar[:], fsq[:], 1.0 / D))
            pl.raw("vector", lambda e: e.drain())
            pl.op("vector", lambda e: e.tensor_mul(mtmp[:], mmu[:], mmu[:]))
            pl.raw("vector", lambda e: e.drain())
            pl.op("vector", lambda e: e.tensor_sub(mvar[:], mvar[:], mtmp[:]))
            pl.raw("vector", lambda e: e.drain())
            pl.op("vector", lambda e: e.tensor_scalar_add(mtmp[:], mvar[:], EPS),
                  inc=(S["mB"], 1))
            pl.wait("scalar", S["mB"], S["mB"].n)
            pl.op("scalar", lambda e: e.activation(mtmp[:], mtmp[:], AF.Sqrt),
                  inc=(S["mB"], 1))
            pl.wait("vector", S["mB"], S["mB"].n)
            pl.op("vector", lambda e: e.reciprocal(mrstd[:], mtmp[:]))
            pl.raw("vector", lambda e: e.drain())
            pl.op("vector", lambda e: e.tensor_mul(mtmp[:], mvar[:], mrstd[:]))
            pl.raw("vector", lambda e: e.drain())
            pl.op("vector", lambda e: e.tensor_mul(mtmp[:], mtmp[:], mrstd[:]))
            pl.raw("vector", lambda e: e.drain())
            pl.op("vector", lambda e: e.tensor_scalar_add(mtmp[:], mtmp[:], EPS),
                  inc=(S["mB"], 1))
            pl.wait("scalar", S["mB"], S["mB"].n)
            pl.op("scalar", lambda e: e.activation(mtmp[:], mtmp[:], AF.Sqrt),
                  inc=(S["mB"], 1))
            pl.wait("vector", S["mB"], S["mB"].n)
            pl.op("vector", lambda e: e.reciprocal(mtmp[:], mtmp[:]))
            pl.raw("vector", lambda e: e.drain())
            pl.op("vector", lambda e: e.tensor_mul(mseff[:], mrstd[:], mtmp[:]))
            pl.raw("vector", lambda e: e.drain())
            pl.op("vector", lambda e: e.tensor_mul(mnb[:], mmu[:], mseff[:]))
            pl.op("vector", lambda e: e.tensor_mul(mnmr[:], mmu[:], mrstd[:]))
            pl.raw("vector", lambda e: e.drain())
            pl.op("vector", lambda e: e.tensor_scalar_mul(mnb[:], mnb[:], -1.0))
            pl.op("vector", lambda e: e.tensor_scalar_mul(mnmr[:], mnmr[:], -1.0),
                  inc=(S["mB"], 1))
            mstats = S["mB"].n

            def mprep(tt):
                if tt == 0:
                    pl.wait("scalar", S["mB"], mstats)
                    pl.wait("scalar", S["gld"], GLDW)
                    pl.wait("scalar", S["xw"], F1_XW)
                if tt >= 1:
                    pl.wait("scalar", S["trp"], C["tr"])
                pl.op("scalar", lambda e, t=tt: e.activation(
                    zz[:, 0:D],
                    xr[:, t * D:(t + 1) * D], AF.Identity,
                    bias=mnb[:, t:t + 1], scale=mseff[:, t:t + 1]),
                    inc=(S["mC"], 1))
                pl.wait("vector", S["mC"], S["mC"].n)
                zv = zz[:, 0:D]
                pl.op("vector", lambda e, z=zv: e.tensor_mul(z, z, wbc[:]))
                pl.op("vector", lambda e, z=zv: e.tensor_add(z, z, bbc[:]),
                      inc=(S["mC"], 1))
                return S["mC"], S["mC"].n

            def msrc(tt, kt):
                return zz[:, kt * P:(kt + 1) * P]

            tre_base2 = emit_transposes(msrc, NT, mprep, F1_XW, 1)

            pl.wait("tensor", S["tre"], tre_base2 + NT * KT)
            pl.wait("tensor", S["gld"], GLDW)
            for tc in range(2):
                for kt in range(KT):
                    pl.op("tensor", lambda e, k=kt, t=tc: e.matmul(
                        pA[0:DIN, t * 512:(t + 1) * 512],
                        inp_s[:, k * DIN:(k + 1) * DIN],
                        lnT[:, k * T + t * 512: k * T + (t + 1) * 512],
                        start=(k == 0), stop=(k == KT - 1)),
                        inc=(S["mD"], 1) if (kt == KT - 1) else None)
            pl.wait("scalar", S["mD"], S["mD"].n)
            pl.op("scalar", lambda e: e.copy(xin[:, 1:T + 1], pA[0:DIN, 0:T]),
                  inc=(S["mD"], 1))
            inproj_done = S["mD"].n

            pl.wait("sync", S["mD"], inproj_done)
            pl.op("sync", lambda e: e.dma_start(
                out=dag1i[:, 0:1], in_=xin[:, 1:2],
                allow_slow_non_contiguous=True), inc=(S["ag"], 16))
            pl.op("sync", lambda e: e.dma_start(
                out=dag1i[:, 1:2], in_=xin[:, T:T + 1],
                allow_slow_non_contiguous=True), inc=(S["ag"], 16))
            pl.wait("gpsimd", S["ag"], S["ag"].n)
            pl.op("gpsimd", lambda e: e.collective_compute(
                "AllGather", ALU.bypass, replica_groups=groups,
                ins=[dag1i], outs=[dag1o]), inc=(S["ag"], 1))
            pl.wait("gpsimd", S["ag"], S["ag"].n)
            pl.op("gpsimd", lambda e: e.dma_start(
                out=slots1[:].rearrange("p (r f) -> p r f", r=4),
                in_=dag1o.rearrange("(r p) f -> p r f", r=4)),
                inc=(S["ag"], 16))
            pl.wait("vector", S["ag"], S["ag"].n)
            pl.op("vector", lambda e: e.memset(xin[:, 0:1], 0.0))
            pl.op("vector", lambda e: e.memset(xin[:, T + 1:T + 2], 0.0))
            pl.raw("vector", lambda e: e.drain())
            sl1 = slots1[:].rearrange("p (r f) -> p r f", r=4)
            for r in range(4):
                pl.op("vector", lambda e, rr=r: e.scalar_tensor_tensor(
                    xin[:, 0:1], sl1[:, rr, 1:2], selL_s[:, rr:rr + 1],
                    xin[:, 0:1], ALU.mult, ALU.add))
                pl.raw("vector", lambda e: e.drain())
                pl.op("vector", lambda e, rr=r: e.scalar_tensor_tensor(
                    xin[:, T + 1:T + 2], sl1[:, rr, 0:1], selR_s[:, rr:rr + 1],
                    xin[:, T + 1:T + 2], ALU.mult, ALU.add))
                pl.raw("vector", lambda e: e.drain())

            pl.op("vector", lambda e: e.tensor_scalar(
                delta[:], xin[:, 0:T], convw_s[:, 0:1], None, ALU.mult))
            pl.op("vector", lambda e: e.scalar_tensor_tensor(
                delta[:], xin[:, 1:T + 1], convw_s[:, 1:2], delta[:],
                ALU.mult, ALU.add))
            pl.op("vector", lambda e: e.scalar_tensor_tensor(
                delta[:], xin[:, 2:T + 2], convw_s[:, 2:3], delta[:],
                ALU.mult, ALU.add), inc=(S["mD"], 1))
            pl.wait("scalar", S["mD"], S["mD"].n)
            pl.op("scalar", lambda e: e.activation(
                u_s[:], delta[:], AF.Silu, bias=convb_s[:]), inc=(S["mD"], 1))
            u_done = S["mD"].n

            pl.wait("tensor", S["mD"], u_done)
            for tc in range(2):
                pl.op("tensor", lambda e, t=tc: e.matmul(
                    pA[0:DST, 1024 + t * 512:1024 + (t + 1) * 512],
                    xpB_s[:], u_s[:, t * 512:(t + 1) * 512],
                    start=True, stop=True),
                    inc=(S["mE"], 1) if tc == 1 else None)
                pl.op("tensor", lambda e, t=tc: e.matmul(
                    pB[0:DST, t * 512:(t + 1) * 512],
                    xpC_s[:], u_s[:, t * 512:(t + 1) * 512],
                    start=True, stop=True),
                    inc=(S["mE"], 1) if tc == 1 else None)
            pl.wait("scalar", S["mE"], S["mE"].n)
            pl.op("scalar", lambda e: e.copy(Bp_s[:], pA[0:DST, 1024:2048]),
                  inc=(S["mE"], 1))
            pl.op("scalar", lambda e: e.copy(Cp_s[:], pB[0:DST, 0:T]),
                  inc=(S["mE"], 1))
            bc_evac = S["mE"].n
            pl.wait("tensor", S["mE"], bc_evac)
            for tc in range(2):
                pl.op("tensor", lambda e, t=tc: e.matmul(
                    pA[0:DTR, 1024 + t * 512:1024 + (t + 1) * 512],
                    xpd_s[:], u_s[:, t * 512:(t + 1) * 512],
                    start=True, stop=True),
                    inc=(S["mE"], 1) if tc == 1 else None)
            pl.wait("scalar", S["mE"], S["mE"].n)
            pl.op("scalar", lambda e: e.copy(xsc[0:DTR, :], pA[0:DTR, 1024:2048]),
                  inc=(S["mE"], 1))
            dr_evac = S["mE"].n
            pl.wait("tensor", S["mE"], dr_evac)
            for tc in range(2):
                pl.op("tensor", lambda e, t=tc: e.matmul(
                    pA[0:DIN, t * 512:(t + 1) * 512],
                    dtT_s[:], xsc[0:DTR, t * 512:(t + 1) * 512],
                    start=True, stop=True),
                    inc=(S["mE"], 1) if tc == 1 else None)
            pl.wait("scalar", S["mE"], S["mE"].n)
            pl.op("scalar", lambda e: e.activation(
                dA0[:], pA[0:DIN, 0:T], AF.Abs, bias=dtb_s[:]))
            pl.op("scalar", lambda e: e.activation(
                dA1[:], dA0[:], AF.Exp, scale=-1.0))
            pl.op("scalar", lambda e: e.activation(
                dA1[:], dA1[:], AF.Ln, bias=1.0))
            pl.op("scalar", lambda e: e.activation(
                dA0[:], pA[0:DIN, 0:T], AF.Relu, bias=dtb_s[:]),
                inc=(S["mE"], 1))
            pl.wait("vector", S["mE"], S["mE"].n)
            pl.op("vector", lambda e: e.tensor_add(delta[:], dA0[:], dA1[:]))
            pl.op("vector", lambda e: e.tensor_scalar_mul(delta[:], delta[:],
                                                          0.001))
            pl.raw("vector", lambda e: e.drain())
            pl.op("vector", lambda e: e.tensor_mul(du_s[:], delta[:], u_s[:]))
            pl.raw("vector", lambda e: e.drain())
            pl.op("vector", lambda e: e.reduce_sum(Stot_s[:], delta[:], axis=AXX))
            pl.raw("vector", lambda e: e.drain())
            pl.op("vector", lambda e: e.tensor_scalar(
                Atot_s[:], negA[:], Stot_s[:], None, ALU.mult), inc=(S["mF"], 1))
            pl.wait("scalar", S["mF"], S["mF"].n)
            pl.op("scalar", lambda e: e.activation(Atot_s[:], Atot_s[:], AF.Exp),
                  inc=(S["mF"], 1))
            prep_done = S["mF"].n

            pl.wait("sync", S["mE"], bc_evac)
            pl.op("sync", lambda e: e.dma_start(out=dbpcp[0:DST, :], in_=Bp_s[:]),
                  inc=(S["bcl"], 16))
            pl.op("sync", lambda e: e.dma_start(out=dbpcp[DST:2 * DST, :],
                                                in_=Cp_s[:]),
                  inc=(S["bcl"], 16))

            nbc = 0
            for n in range(DST):
                if nbc >= 2:
                    pl.wait("sync", S["bcf"], nbc - 1)
                bslot = bcB[:, (n % 2) * T:((n % 2) + 1) * T]
                pl.op("sync", lambda e, d_=bslot, r=n: e.dma_start(
                    out=d_, in_=dbpcp[r:r + 1, :].broadcast_to([DIN, T])),
                    inc=(S["bcl"], 16))
                nbc += 1
                if n >= 2:
                    pl.wait("scalar", S["mG"], 3 * (n - 2) + 2)
                pl.op("scalar", lambda e, r=n, b=dAbufs[n % 2]: e.activation(
                    b[:], delta[:], AF.Exp, scale=negA[:, r:r + 1]),
                    inc=(S["mF"], 1))
                pl.wait("vector", S["mF"], prep_done + n + 1)
                pl.wait("vector", S["bcl"], 32 + 16 * nbc)
                pl.op("vector", lambda e, b=bslot: e.tensor_mul(dBu[:], du_s[:],
                                                                b),
                      inc=(S["bcf"], 1), inc2=(S["mG"], 1))
                pl.op("vector", lambda e, b=dAbufs[n % 2]: e.tensor_tensor_scan(
                    xsc[:], b[:], dBu[:], 0.0, ALU.mult, ALU.add),
                    inc=(S["mG"], 1))
                pl.raw("vector", lambda e: e.drain())
                pl.op("vector", lambda e, r=n: e.tensor_copy(
                    F_s[:, r:r + 1], xsc[:, T - 1:T]), inc=(S["mG"], 1))
                pl.raw("vector", lambda e: e.drain())
            pass1_done = S["mG"].n

            pl.wait("sync", S["mG"], pass1_done)
            pl.op("sync", lambda e: e.dma_start(out=dag2i[:, 0:DST], in_=F_s[:]),
                  inc=(S["ag"], 16))
            pl.op("sync", lambda e: e.dma_start(out=dag2i[:, DST:2 * DST],
                                                in_=Atot_s[:]),
                  inc=(S["ag"], 16))
            pl.wait("gpsimd", S["ag"], S["ag"].n)
            pl.op("gpsimd", lambda e: e.collective_compute(
                "AllGather", ALU.bypass, replica_groups=groups,
                ins=[dag2i], outs=[dag2o]), inc=(S["ag"], 1))
            pl.wait("gpsimd", S["ag"], S["ag"].n)
            pl.op("gpsimd", lambda e: e.dma_start(
                out=slots2[:].rearrange("p (r f) -> p r f", r=4),
                in_=dag2o.rearrange("(r p) f -> p r f", r=4)),
                inc=(S["ag"], 16))
            pl.wait("vector", S["ag"], S["ag"].n)
            sl2 = slots2[:].rearrange("p (r f) -> p r f", r=4)
            pl.op("vector", lambda e: e.tensor_copy(cs1[:], sl2[:, 0, 0:DST]))
            pl.raw("vector", lambda e: e.drain())
            pl.op("vector", lambda e: e.tensor_mul(ctmp[:],
                                                   sl2[:, 1, DST:2 * DST], cs1[:]))
            pl.raw("vector", lambda e: e.drain())
            pl.op("vector", lambda e: e.tensor_add(cs2[:], ctmp[:],
                                                   sl2[:, 1, 0:DST]))
            pl.raw("vector", lambda e: e.drain())
            pl.op("vector", lambda e: e.tensor_mul(ctmp[:],
                                                   sl2[:, 2, DST:2 * DST], cs2[:]))
            pl.raw("vector", lambda e: e.drain())
            pl.op("vector", lambda e: e.tensor_add(cs3[:], ctmp[:],
                                                   sl2[:, 2, 0:DST]))
            pl.raw("vector", lambda e: e.drain())
            pl.op("vector", lambda e: e.memset(carry[:], 0.0))
            pl.raw("vector", lambda e: e.drain())
            for j, csj in enumerate((cs1, cs2, cs3)):
                pl.op("vector", lambda e, c=csj, jj=j: e.scalar_tensor_tensor(
                    carry[:], c[:], selc_s[:, jj:jj + 1], carry[:],
                    ALU.mult, ALU.add))
                pl.raw("vector", lambda e: e.drain())
            pl.op("vector", lambda e: e.memset(y_s[:], 0.0), inc=(S["mG"], 1))
            mg_memset = S["mG"].n

            for n in range(DST):
                if nbc >= 2:
                    pl.wait("sync", S["bcf"], nbc - 1)
                bslot = bcB[:, (n % 2) * T:((n % 2) + 1) * T]
                pl.op("sync", lambda e, d_=bslot, r=n: e.dma_start(
                    out=d_, in_=dbpcp[r:r + 1, :].broadcast_to([DIN, T])),
                    inc=(S["bcl"], 16))
                nbc += 1
                if nbc >= 2:
                    pl.wait("sync", S["bcf"], nbc - 1)
                cslot = bcC[:, (n % 2) * T:((n % 2) + 1) * T]
                pl.op("sync", lambda e, d_=cslot, r=n: e.dma_start(
                    out=d_,
                    in_=dbpcp[DST + r:DST + r + 1, :].broadcast_to([DIN, T])),
                    inc=(S["bcl"], 16))
                nbc += 1
                if n >= 2:
                    pl.wait("scalar", S["mH"], 4 * (n - 2) + 2)
                pl.op("scalar", lambda e, r=n, b=dAbufs[n % 2]: e.activation(
                    b[:], delta[:], AF.Exp, scale=negA[:, r:r + 1]),
                    inc=(S["mF"], 1))
                pl.wait("vector", S["mF"], prep_done + DST + n + 1)
                pl.wait("vector", S["bcl"], 32 + 16 * nbc)
                pl.op("vector", lambda e, b=bslot: e.tensor_mul(dBu[:], du_s[:],
                                                                b),
                      inc=(S["bcf"], 1), inc2=(S["mH"], 1))
                pl.op("vector", lambda e, b=dAbufs[n % 2], r=n:
                      e.tensor_tensor_scan(
                          xsc[:], b[:], dBu[:], carry[:, r:r + 1],
                          ALU.mult, ALU.add),
                      inc=(S["mH"], 1))
                pl.op("vector", lambda e, c=cslot: e.tensor_mul(dBu[:], xsc[:], c),
                      inc=(S["bcf"], 1), inc2=(S["mH"], 1))
                pl.op("vector", lambda e: e.tensor_add(y_s[:], y_s[:], dBu[:]),
                      inc=(S["mH"], 1))
            pl.op("vector", lambda e: e.scalar_tensor_tensor(
                y_s[:], u_s[:], Dp_s[:], y_s[:], ALU.mult, ALU.add))
            pl.op("vector", lambda e: e.tensor_copy(y16[:], y_s[:]),
                  inc=(S["mH"], 1))
            y_done = S["mH"].n

            pl.wait("tensor", S["mH"], y_done)
            for tt in range(NT):
                for q in range(4):
                    pl.op("tensor", lambda e, t=tt, qq=q: e.matmul(
                        pA[:, qq * 512:(qq + 1) * 512],
                        y16[:, t * P:(t + 1) * P],
                        outT_s[:, qq * 512:(qq + 1) * 512],
                        start=True, stop=True),
                        inc=(S["mH"], 1) if q == 3 else None)
                pl.wait("vector", S["mH"], S["mH"].n)
                xv = xr[:, tt * D:(tt + 1) * D]
                pl.op("vector", lambda e, x=xv, t=tt: e.scalar_tensor_tensor(
                    x, x, mrstd[:, t:t + 1], x, ALU.mult, ALU.add))
                pl.op("vector", lambda e, x=xv, t=tt: e.tensor_scalar(
                    x, x, mnmr[:, t:t + 1], None, ALU.add))
                vres = pl.op("vector", lambda e, x=xv: e.tensor_add(
                    x, x, pA[:, 0:D]), inc=(S["xw"], 1))
                pl.wait("tensor", S["xw"], vres)
            for g in reversed(m_guards):
                g.__exit__(None, None, None)
            mamba_xw = S["xw"].n

        if do_ffn2:
            g_tok2 = nc.sbuf_tensor("tok2", [P, 2 * D], BF16)
            g_wd2 = nc.sbuf_tensor("wdA2", [P, IT * D], BF16)
            state["ln1tok"] = g_tok2.__enter__()
            state["wd"] = g_wd2.__enter__()
            ffn_call(1, xw_entry=mamba_xw, rcp_entry=1,
                     xw_tile=(lambda tt, base=mamba_xw: base - NT + tt + 1)
                     if do_mamba else None)
            g_wd2.__exit__(None, None, None)
            g_tok2.__exit__(None, None, None)

        if do_ffn2:
            for tt in range(NT):
                pl.wait("sync", S["xw"], state["tile_last_stt"][(1, tt)])
                pl.op("sync", lambda e, t=tt: e.dma_start(
                    out=dy[t * P:(t + 1) * P, :], in_=xr[:, t * D:(t + 1) * D]),
                    inc=(S["st"], 16))
        else:
            pl.wait("sync", S["xw"], S["xw"].n)
            for tt in range(NT):
                pl.op("sync", lambda e, t=tt: e.dma_start(
                    out=dy[t * P:(t + 1) * P, :], in_=xr[:, t * D:(t + 1) * D]),
                    inc=(S["st"], 16))
        pl.wait("sync", S["st"], S["st"].n)

        with nc.Block() as block:
            block.tensor(lambda e: [f(e) for f in pl.q["tensor"]])
            block.vector(lambda e: [f(e) for f in pl.q["vector"]])
            block.scalar(lambda e: [f(e) for f in pl.q["scalar"]])
            block.sync(lambda e: [f(e) for f in pl.q["sync"]])
            block.gpsimd(lambda e: [f(e) for f in pl.q["gpsimd"]])

    return nc


def prep_inputs(inputs):
    f = np.ascontiguousarray
    w = {}
    eg = inputs["L0_ffn_expert_gate_w"]
    eu = inputs["L0_ffn_expert_up_w"]
    ed = inputs["L0_ffn_expert_down_w"]
    sg = inputs["L0_ffn_shared_gate_w"]
    su = inputs["L0_ffn_shared_up_w"]
    sd = inputs["L0_ffn_shared_down_w"]
    wg_dmaj = np.concatenate([eg.transpose(0, 2, 1), sg.T[None]], axis=0)
    wu_dmaj = np.concatenate([eu.transpose(0, 2, 1), su.T[None]], axis=0)
    w["wgT"] = f(wg_dmaj.reshape(NE, KT, P, IT, P).transpose(0, 3, 2, 1, 4))
    w["wuT"] = f(wu_dmaj.reshape(NE, KT, P, IT, P).transpose(0, 3, 2, 1, 4))
    w["wdT"] = f(np.concatenate([ed.transpose(0, 2, 1), sd.T[None]], axis=0))
    w["gateT"] = f(inputs["L0_ffn_gate_weight"].T)
    w["inpT"] = f(inputs["L0_attn_in_proj_W"].T)
    xp = inputs["L0_attn_x_proj_W"]
    w["xpd"] = f(xp[0:DTR].T)
    w["xpB"] = f(xp[DTR:DTR + DST].T)
    w["xpC"] = f(xp[DTR + DST:DTR + 2 * DST].T)
    w["dtT"] = f(inputs["L0_attn_dt_proj_W"].T)
    w["dtb"] = f(inputs["L0_attn_dt_proj_b"].reshape(DIN, 1))
    w["convw"] = f(inputs["L0_attn_conv_w"].reshape(DIN, 3))
    w["convb"] = f(inputs["L0_attn_conv_b"].reshape(DIN, 1))
    w["Alog"] = f(inputs["L0_attn_A_log"])
    w["Dp"] = f(inputs["L0_attn_D_param"].reshape(DIN, 1))
    w["outT"] = f(inputs["L0_attn_out_proj_W"].T)
    w["lnw_bc"] = f(np.broadcast_to(np.asarray(inputs["L0_attn_ln_w"])[None, :],
                                    (P, D)))
    w["lnb_bc"] = f(np.broadcast_to(np.asarray(inputs["L0_attn_ln_b"])[None, :],
                                    (P, D)))
    w["ident"] = np.eye(P, dtype=np.float32)
    w["eiota"] = np.ascontiguousarray(
        np.tile(np.arange(8, dtype=np.float32)[None, :], (P, NT)))
    w["tokid"] = np.ascontiguousarray(
        (np.arange(NT, dtype=np.float32)[None, :] * P
         + np.arange(P, dtype=np.float32)[:, None]))
    for k in w:
        w[k] = np.asarray(w[k], dtype=np.float32)
    return w


def make_in_maps(x, w):
    in_maps = []
    for c in range(NCORES):
        b, ch = c // 4, c % 4
        m = dict(w)
        m["x"] = np.ascontiguousarray(x[b, ch * T:(ch + 1) * T, :])
        selc = np.zeros((DIN, 3), np.float32)
        if ch >= 1:
            selc[:, ch - 1] = 1.0
        selL = np.zeros((DIN, 4), np.float32)
        selR = np.zeros((DIN, 4), np.float32)
        if ch > 0:
            selL[:, ch - 1] = 1.0
        if ch < 3:
            selR[:, ch + 1] = 1.0
        m["selc"] = selc
        m["selL"] = selL
        m["selR"] = selR
        in_maps.append(m)
    return in_maps


def kernel(**inputs):
    x = np.asarray(inputs["L0_attn_feat_patch"], dtype=np.float32)
    w = prep_inputs(inputs)
    nc = build_program()
    in_maps = make_in_maps(x, w)
    res = run_bass_kernel_spmd(nc, in_maps, core_ids=list(range(NCORES)))
    out = np.empty((2, 4096, D), dtype=np.float32)
    for c in range(NCORES):
        b, ch = c // 4, c % 4
        out[b, ch * T:(ch + 1) * T, :] = np.asarray(res.results[c]["y"],
                                                    dtype=np.float32)
    return out


if __name__ == "__main__":
    import reference as _r
    inp = _r.setup_inputs()
    got = kernel(**{k: np.asarray(v) for k, v in inp.items()})
    want = np.asarray(_r.reference(**inp))
    err = np.linalg.norm(got - want) / np.linalg.norm(want)
    print("rel_err", err)
